# version 03 (final)
# baseline (speedup 1.0000x reference)
import contextlib
import numpy as np
import ml_dtypes
import concourse.bass as bass
import concourse.mybir as mybir
from concourse.bass_utils import run_bass_kernel_spmd

F32 = mybir.dt.float32
BF16 = mybir.dt.bfloat16
AF = mybir.ActivationFunctionType
OP = mybir.AluOpType
AX = mybir.AxisListType

T = 4096
D = 1024
NT = T // 128
NS = T // 512
NC_ = D // 128
STG = 6144

ENGS = ("sync", "tensor", "vector", "scalar", "gpsimd")
DMA_ENGS = ("sync", "gpsimd")


class Prog:

    def __init__(self, nc):
        self.nc = nc
        self.cmds = {e: [] for e in ENGS}
        self.cnt = {e: 0 for e in ENGS}
        self.waited = {e: {e2: 0 for e2 in ENGS} for e in ENGS}
        self.lastw = {}
        self.lastr = {}
        self.sems = None
        self.epoch = 0

    def _wait(self, eng, src, val):
        if val <= 0:
            return
        if src == eng == "tensor":
            return
        if self.waited[eng][src] >= val:
            return
        self.waited[eng][src] = val
        self.cmds[eng].append(("w", src, val, self.epoch))

    def op(self, eng, fn, reads=(), writes=()):
        if getattr(self, "mute", False):
            return
        for k in reads:
            w = self.lastw.get(k)
            if w is not None:
                self._wait(eng, w[0], w[1])
        for k in writes:
            w = self.lastw.get(k)
            if w is not None:
                self._wait(eng, w[0], w[1])
            for e2, cv in self.lastr.get(k, {}).items():
                self._wait(eng, e2, cv)
        inc = 16 if eng in DMA_ENGS else 1
        self.cnt[eng] += inc
        self.cmds[eng].append(("i", fn, self.epoch))
        for k in writes:
            self.lastw[k] = (eng, self.cnt[eng])
            self.lastr.pop(k, None)
        for k in reads:
            self.lastr.setdefault(k, {})[eng] = self.cnt[eng]

    def barrier(self):
        snap = dict(self.cnt)
        for e in ENGS:
            for s in ENGS:
                self._wait(e, s, snap[s])

    def epoch_break(self):
        self.barrier()
        self.epoch += 1
        self.cnt = {e: 0 for e in ENGS}
        self.waited = {e: {e2: 0 for e2 in ENGS} for e in ENGS}
        self.lastw.clear()
        self.lastr.clear()

    def replay(self, block):
        bmap = {}
        for e in DMA_ENGS:
            ends = {}
            run = []
            c = 0
            ep = 0
            for cmd in self.cmds[e]:
                if cmd[-1] != ep:
                    for v in run:
                        ends[(ep, v)] = run[-1]
                    run = []
                    c = 0
                    ep = cmd[-1]
                if cmd[0] == "w":
                    for v in run:
                        ends[(ep, v)] = run[-1]
                    run = []
                else:
                    c += 16
                    run.append(c)
            for v in run:
                ends[(ep, v)] = run[-1]
            bmap[e] = ends

        def mk(engname):
            def section(E):
                count = 0
                drained = 0
                ep = 0
                for c in self.cmds[engname]:
                    if c[-1] != ep:
                        ep = c[-1]
                        count = 0
                        drained = 0
                    if c[0] == "w":
                        if engname in DMA_ENGS and count > drained:
                            E.wait_ge(self.sems[(ep, engname)], count)
                            drained = count
                        val = c[2]
                        if c[1] in DMA_ENGS:
                            val = bmap[c[1]].get((ep, val), val)
                        E.wait_ge(self.sems[(ep, c[1])], val)
                    else:
                        inst = c[1](E)
                        inc = 16 if engname in DMA_ENGS else 1
                        count += inc
                        inst.then_inc(self.sems[(ep, engname)], inc)
            return section
        block.sync(mk("sync"))
        block.tensor(mk("tensor"))
        block.vector(mk("vector"))
        block.scalar(mk("scalar"))
        block.gpsimd(mk("gpsimd"))



def load_weightT(P, w_hbm, wbf, wkey, nch, fout, wst, wstk):
    sch = max(1, STG // fout)
    c = 0
    while c < nch:
        cs = list(range(c, min(c + sch, nch)))
        for k, cc in enumerate(cs):
            P.op("sync", lambda E, cc=cc, k=k: E.dma_start(
                out=wst[:, k * fout:(k + 1) * fout],
                in_=w_hbm[cc * 128:(cc + 1) * 128, :]),
                writes=(wstk,))
        P.op("scalar", lambda E, c=c, n=len(cs): E.activation(
            out=wbf[:, c * fout:(c + n) * fout], in_=wst[:, 0:n * fout], func=AF.Copy),
            reads=(wstk,), writes=(wkey,))
        c += sch


def mk_ln_scratch(es, nc, tag):
    sm = es.enter_context(nc.sbuf_tensor(tag + "sm", [128, 1], F32))
    xc = es.enter_context(nc.sbuf_tensor(tag + "xc", [128, D], F32))
    sq = es.enter_context(nc.sbuf_tensor(tag + "sq", [128, D], F32))
    varv = es.enter_context(nc.sbuf_tensor(tag + "var", [128, 1], F32))
    stdv = es.enter_context(nc.sbuf_tensor(tag + "std", [128, 1], F32))
    rstd = es.enter_context(nc.sbuf_tensor(tag + "rstd", [128, 1], F32))
    return (sm, xc, sq, varv, stdv, rstd)


def emit_ln(P, xs, xs_key, j, out_bf, out_key, scr, sk):
    xj = xs[:, j * D:(j + 1) * D]
    sm, xc, sq, varv, stdv, rstd = scr
    P.op("vector", lambda E: E.reduce_sum(out=sm[:, :], in_=xj, axis=AX.X),
         reads=(xs_key,), writes=(sk + "sm",))
    P.op("scalar", lambda E: E.mul(out=sm[:, :], in_=sm[:, :], mul=1.0 / D),
         reads=(sk + "sm",), writes=(sk + "sm",))
    P.op("vector", lambda E: E.tensor_scalar_sub(out=xc[:, :], in0=xj, scalar1=sm[:, :]),
         reads=(xs_key, sk + "sm"), writes=(sk + "xc",))
    P.op("scalar", lambda E: E.activation(out=sq[:, :], in_=xc[:, :], func=AF.Square),
         reads=(sk + "xc",), writes=(sk + "sq",))
    P.op("vector", lambda E: E.reduce_sum(out=varv[:, :], in_=sq[:, :], axis=AX.X),
         reads=(sk + "sq",), writes=(sk + "var",))
    P.op("vector", lambda E: E.tensor_scalar(out=varv[:, :], in0=varv[:, :],
         scalar1=1.0 / D, scalar2=1e-5, op0=OP.mult, op1=OP.add),
         reads=(sk + "var",), writes=(sk + "var",))
    P.op("scalar", lambda E: E.activation(out=stdv[:, :], in_=varv[:, :], func=AF.Sqrt),
         reads=(sk + "var",), writes=(sk + "std",))
    P.op("vector", lambda E: E.reciprocal(out=rstd[:, :], in_=stdv[:, :]),
         reads=(sk + "std",), writes=(sk + "rstd",))
    P.op("vector", lambda E: E.tensor_scalar_mul(out=out_bf, in0=xc[:, :], scalar1=rstd[:, :]),
         reads=(sk + "xc", sk + "rstd"), writes=(out_key,))


def emit_transposes(P, ln_bf, ln_key, identb, psT, psT_keys, dest_fn):
    n = len(psT)
    for c in range(NC_):
        i = c % n
        P.op("tensor", lambda E, c=c, i=i: E.transpose(
            out=psT[i][:, :], in_=ln_bf[:, c * 128:(c + 1) * 128], identity=identb[:, :]),
            reads=(ln_key, "identb"), writes=(psT_keys[i],))
        dap, dkey = dest_fn(c)
        P.op("vector", lambda E, dap=dap, i=i: E.tensor_copy(out=dap, in_=psT[i][:, :]),
             reads=(psT_keys[i],), writes=(dkey,))


def bias_bcast(P, es, nc, b_hbm, name, width, ones_f):
    row = es.enter_context(nc.sbuf_tensor(name + "_row", [1, width], F32))
    bc = es.enter_context(nc.sbuf_tensor(name + "_bc", [128, width], F32))
    pb = es.enter_context(nc.psum_tensor(name + "_pb", [128, 512], F32))
    P.op("sync", lambda E: E.dma_start(out=row[:, :], in_=b_hbm[None, :]),
         writes=(name + "_row",))
    for fb in range(width // 512):
        P.op("tensor", lambda E, fb=fb: E.matmul(
            out=pb[:, :], lhsT=ones_f[0:1, 0:128], rhs=row[0:1, fb * 512:(fb + 1) * 512],
            start=True, stop=True),
            reads=(name + "_row", "ones"), writes=(name + "_pb",))
        P.op("vector", lambda E, fb=fb: E.tensor_copy(
            out=bc[:, fb * 512:(fb + 1) * 512], in_=pb[:, :]),
            reads=(name + "_pb",), writes=(name + "_bc",))
    return bc


def emit_softmax(P, pfx, ens, ensk, mx, den, attb, attbk, scale, nk=None):
    esl = ens[:, :] if nk is None else ens[:, 0:nk]
    asl = attb[:, :] if nk is None else attb[:, 0:nk]
    P.op("vector", lambda E, esl=esl: E.reduce_max(out=mx[:, :], in_=esl, axis=AX.X),
         reads=(ensk,), writes=(pfx + "mx",))
    P.op("vector", lambda E, esl=esl: E.tensor_scalar(
        out=esl, in0=esl, scalar1=mx[:, :], scalar2=scale, op0=OP.subtract, op1=OP.mult),
        reads=(ensk, pfx + "mx"), writes=(ensk,))
    P.op("scalar", lambda E, esl=esl: E.activation(out=esl, in_=esl, func=AF.Exp),
         reads=(ensk,), writes=(ensk,))
    P.op("vector", lambda E, esl=esl: E.reduce_sum(out=den[:, :], in_=esl, axis=AX.X),
         reads=(ensk,), writes=(pfx + "den",))
    P.op("vector", lambda E: E.reciprocal(out=den[:, :], in_=den[:, :]),
         reads=(pfx + "den",), writes=(pfx + "den",))
    P.op("vector", lambda E, esl=esl, asl=asl: E.tensor_scalar_mul(
        out=asl, in0=esl, scalar1=den[:, :]),
        reads=(ensk, pfx + "den"), writes=(attbk,))


def emit_ln_strip(P, x, xs, xk, lnT, lk, st, scr, sk, lnb, lnbk, identb, psT, psT_keys):
    for j in range(4):
        P.op("sync", lambda E, st=st, j=j: E.dma_start(
            out=xs[:, j * D:(j + 1) * D],
            in_=x[st * 512 + j * 128: st * 512 + (j + 1) * 128, :]),
            writes=(xk,))
    for j in range(4):
        emit_ln(P, xs, xk, j, lnb[:, :], lnbk, scr, sk)
        emit_transposes(P, lnb, lnbk, identb, psT, psT_keys,
                        lambda c, j=j: (lnT[:, c * 512 + j * 128: c * 512 + (j + 1) * 128], lk))



def emit_stage_a(nc, P, es, x, xkey, y, ykey, wq, wk, wv, wp, bq_pc, bk_pc, bv, bp,
                 masks, identb, ones_f, pfx="a"):
    qfm = nc.dram_tensor(pfx + "_qfm", [D, T], BF16, kind="Internal").ap()
    kfm = nc.dram_tensor(pfx + "_kfm", [D, T], BF16, kind="Internal").ap()
    vtm = nc.dram_tensor(pfx + "_vtm", [T, D], BF16, kind="Internal").ap()
    hk = pfx + "hbm"

    with contextlib.ExitStack() as s1:
        wst = s1.enter_context(nc.sbuf_tensor(pfx + "wst", [128, STG], F32))
        wqb = s1.enter_context(nc.sbuf_tensor(pfx + "wqb", [128, NC_ * D], BF16))
        wkb = s1.enter_context(nc.sbuf_tensor(pfx + "wkb", [128, NC_ * D], BF16))
        wvb = s1.enter_context(nc.sbuf_tensor(pfx + "wvb", [128, NC_ * D], BF16))
        load_weightT(P, wq, wqb, pfx + "wqb", NC_, D, wst, pfx + "wst")
        load_weightT(P, wk, wkb, pfx + "wkb", NC_, D, wst, pfx + "wst")
        load_weightT(P, wv, wvb, pfx + "wvb", NC_, D, wst, pfx + "wst")
        bqs = s1.enter_context(nc.sbuf_tensor(pfx + "bqs", [128, NC_], F32))
        bks = s1.enter_context(nc.sbuf_tensor(pfx + "bks", [128, NC_], F32))
        P.op("sync", lambda E: E.dma_start(out=bqs[:, :], in_=bq_pc[:, :]), writes=(pfx + "bqs",))
        P.op("sync", lambda E: E.dma_start(out=bks[:, :], in_=bk_pc[:, :]), writes=(pfx + "bks",))
        bv_bc = bias_bcast(P, s1, nc, bv, pfx + "bv", D, ones_f)

        xs = [s1.enter_context(nc.sbuf_tensor(f"{pfx}xs{i}", [128, 4 * D], F32)) for i in range(2)]
        lnb = s1.enter_context(nc.sbuf_tensor(pfx + "lnb", [128, D], BF16))
        lnT = [s1.enter_context(nc.sbuf_tensor(f"{pfx}lnT{i}", [128, NC_ * 512], BF16)) for i in range(2)]
        scr = mk_ln_scratch(s1, nc, pfx + "ln")
        psT = [s1.enter_context(nc.psum_tensor(f"{pfx}psT{i}", [128, 128], BF16)) for i in range(2)]
        pq = s1.enter_context(nc.psum_tensor(pfx + "pq", [128, 512], F32))
        pv = s1.enter_context(nc.psum_tensor(pfx + "pv", [128, 512], F32))
        ev = [s1.enter_context(nc.sbuf_tensor(f"{pfx}ev{i}", [128, 512], BF16)) for i in range(2)]

        for st in range(NS):
            b = st % 2
            xk, lk = f"{pfx}xs{b}", f"{pfx}lnT{b}"
            emit_ln_strip(P, x, xs[b], xk, lnT[b], lk, st, scr, pfx + "ln",
                          lnb, pfx + "lnb", identb, psT, [f"{pfx}psT0", f"{pfx}psT1"])
            for (wb, wkey, bcol, bkey, dst) in ((wqb, pfx + "wqb", bqs, pfx + "bqs", qfm),
                                                (wkb, pfx + "wkb", bks, pfx + "bks", kfm)):
                for fc in range(NC_):
                    for ec in range(NC_):
                        P.op("tensor", lambda E, fc=fc, ec=ec, wb=wb, b=b: E.matmul(
                            out=pq[:, :], lhsT=wb[:, ec * D + fc * 128: ec * D + fc * 128 + 128],
                            rhs=lnT[b][:, ec * 512:(ec + 1) * 512],
                            start=(ec == 0), stop=(ec == NC_ - 1)),
                            reads=(wkey, lk), writes=(pfx + "pq",))
                    eb = fc % 2
                    P.op("vector", lambda E, fc=fc, eb=eb, bcol=bcol: E.tensor_scalar_add(
                        out=ev[eb][:, :], in0=pq[:, :], scalar1=bcol[:, fc:fc + 1]),
                        reads=(pfx + "pq", bkey), writes=(f"{pfx}ev{eb}",))
                    P.op("gpsimd", lambda E, fc=fc, eb=eb, st=st, dst=dst: E.dma_start(
                        out=dst[fc * 128:(fc + 1) * 128, st * 512:(st + 1) * 512], in_=ev[eb][:, :]),
                        reads=(f"{pfx}ev{eb}",), writes=(hk,))
            for tj in range(4):
                for fb in range(2):
                    for ec in range(NC_):
                        P.op("tensor", lambda E, tj=tj, fb=fb, ec=ec, b=b: E.matmul(
                            out=pv[:, :], lhsT=lnT[b][:, ec * 512 + tj * 128: ec * 512 + tj * 128 + 128],
                            rhs=wvb[:, ec * D + fb * 512: ec * D + fb * 512 + 512],
                            start=(ec == 0), stop=(ec == NC_ - 1)),
                            reads=(pfx + "wvb", lk), writes=(pfx + "pv",))
                    eb = (tj * 2 + fb) % 2
                    P.op("vector", lambda E, fb=fb, eb=eb: E.tensor_tensor(
                        out=ev[eb][:, :], in0=pv[:, :], in1=bv_bc[:, fb * 512:(fb + 1) * 512], op=OP.add),
                        reads=(pfx + "pv", pfx + "bv_bc"), writes=(f"{pfx}ev{eb}",))
                    P.op("gpsimd", lambda E, st=st, tj=tj, fb=fb, eb=eb: E.dma_start(
                        out=vtm[st * 512 + tj * 128: st * 512 + (tj + 1) * 128, fb * 512:(fb + 1) * 512],
                        in_=ev[eb][:, :]),
                        reads=(f"{pfx}ev{eb}",), writes=(hk,))
    P.barrier()

    with contextlib.ExitStack() as s2:
        wst = s2.enter_context(nc.sbuf_tensor(pfx + "wst2", [128, STG], F32))
        wpb = s2.enter_context(nc.sbuf_tensor(pfx + "wpb", [128, NC_ * D], BF16))
        load_weightT(P, wp, wpb, pfx + "wpb", NC_, D, wst, pfx + "wst2")
        bp_bc = bias_bcast(P, s2, nc, bp, pfx + "bp", D, ones_f)
        msk = s2.enter_context(nc.sbuf_tensor(pfx + "msk", [128, 3 * 384], F32))
        P.op("sync", lambda E: E.dma_start(out=msk[:, :], in_=masks[:, :]), writes=(pfx + "msk",))

        qf = [s2.enter_context(nc.sbuf_tensor(f"{pfx}qf{i}", [128, NC_ * 128], BF16)) for i in range(2)]
        kf = [s2.enter_context(nc.sbuf_tensor(f"{pfx}kf{i}", [128, NC_ * 384], BF16)) for i in range(2)]
        vt = [s2.enter_context(nc.sbuf_tensor(f"{pfx}vt{i}", [128, 3 * D], BF16)) for i in range(2)]
        xr = [s2.enter_context(nc.sbuf_tensor(f"{pfx}xr{i}", [128, D], F32)) for i in range(2)]
        pen = s2.enter_context(nc.psum_tensor(pfx + "pen", [128, 384], F32))
        ens = s2.enter_context(nc.sbuf_tensor(pfx + "ens", [128, 384], F32))
        mx = s2.enter_context(nc.sbuf_tensor(pfx + "mx", [128, 1], F32))
        den = s2.enter_context(nc.sbuf_tensor(pfx + "den", [128, 1], F32))
        attb = s2.enter_context(nc.sbuf_tensor(pfx + "attb", [128, 384], BF16))
        attT = s2.enter_context(nc.sbuf_tensor(pfx + "attT", [128, 3 * 128], BF16))
        psU = [s2.enter_context(nc.psum_tensor(f"{pfx}psU{i}", [128, 128], BF16)) for i in range(2)]
        pctx = s2.enter_context(nc.psum_tensor(pfx + "pctx", [64, 128], F32))
        ctxT = s2.enter_context(nc.sbuf_tensor(pfx + "ctxT", [128, NC_ * 128], BF16))
        po = s2.enter_context(nc.psum_tensor(pfx + "po", [128, 512], F32))
        oev = s2.enter_context(nc.sbuf_tensor(pfx + "oev", [128, D], F32))

        for ti in range(NT):
            b = ti % 2
            w0 = min(max(ti - 1, 0), NT - 3)
            moff = ti - w0
            for c in range(NC_):
                P.op("sync", lambda E, ti=ti, c=c, b=b: E.dma_start(
                    out=qf[b][:, c * 128:(c + 1) * 128],
                    in_=qfm[c * 128:(c + 1) * 128, ti * 128:(ti + 1) * 128]),
                    reads=(hk,), writes=(f"{pfx}qf{b}",))
                P.op("sync", lambda E, w0=w0, c=c, b=b: E.dma_start(
                    out=kf[b][:, c * 384:(c + 1) * 384],
                    in_=kfm[c * 128:(c + 1) * 128, w0 * 128: w0 * 128 + 384]),
                    reads=(hk,), writes=(f"{pfx}kf{b}",))
            for j3 in range(3):
                P.op("sync", lambda E, w0=w0, j3=j3, b=b: E.dma_start(
                    out=vt[b][:, j3 * D:(j3 + 1) * D],
                    in_=vtm[w0 * 128 + j3 * 128: w0 * 128 + (j3 + 1) * 128, :]),
                    reads=(hk,), writes=(f"{pfx}vt{b}",))
            P.op("sync", lambda E, ti=ti, b=b: E.dma_start(
                out=xr[b][:, :], in_=x[ti * 128:(ti + 1) * 128, :]),
                reads=(xkey,), writes=(f"{pfx}xr{b}",))
            for h in range(16):
                c, o = h // 2, (h % 2) * 64
                P.op("tensor", lambda E, c=c, o=o, b=b: E.matmul(
                    out=pen[:, :], lhsT=qf[b][o:o + 64, c * 128:(c + 1) * 128],
                    rhs=kf[b][o:o + 64, c * 384:(c + 1) * 384], start=True, stop=True),
                    reads=(f"{pfx}qf{b}", f"{pfx}kf{b}"), writes=(pfx + "pen",))
                P.op("vector", lambda E, moff=moff: E.tensor_tensor(
                    out=ens[:, :], in0=pen[:, :], in1=msk[:, moff * 384:(moff + 1) * 384], op=OP.add),
                    reads=(pfx + "pen", pfx + "msk"), writes=(pfx + "ens",))
                emit_softmax(P, pfx, ens, pfx + "ens", mx, den, attb, pfx + "attb", 1.0 / 32.0)
                for k3 in range(3):
                    i = k3 % 2
                    P.op("tensor", lambda E, k3=k3, i=i: E.transpose(
                        out=psU[i][:, :], in_=attb[:, k3 * 128:(k3 + 1) * 128], identity=identb[:, :]),
                        reads=(pfx + "attb", "identb"), writes=(f"{pfx}psU{i}",))
                    P.op("vector", lambda E, k3=k3, i=i: E.tensor_copy(
                        out=attT[:, k3 * 128:(k3 + 1) * 128], in_=psU[i][:, :]),
                        reads=(f"{pfx}psU{i}",), writes=(pfx + "attT",))
                for k3 in range(3):
                    P.op("tensor", lambda E, k3=k3, h=h, b=b: E.matmul(
                        out=pctx[:, :], lhsT=vt[b][:, k3 * D + h * 64: k3 * D + h * 64 + 64],
                        rhs=attT[:, k3 * 128:(k3 + 1) * 128], start=(k3 == 0), stop=(k3 == 2)),
                        reads=(f"{pfx}vt{b}", pfx + "attT"), writes=(pfx + "pctx",))
                P.op("vector", lambda E, c=c, o=o: E.tensor_copy(
                    out=ctxT[o:o + 64, c * 128:(c + 1) * 128], in_=pctx[:, :]),
                    reads=(pfx + "pctx",), writes=(pfx + "ctxT",))
            for fb in range(2):
                for ec in range(NC_):
                    P.op("tensor", lambda E, fb=fb, ec=ec: E.matmul(
                        out=po[:, :], lhsT=ctxT[:, ec * 128:(ec + 1) * 128],
                        rhs=wpb[:, ec * D + fb * 512: ec * D + fb * 512 + 512],
                        start=(ec == 0), stop=(ec == NC_ - 1)),
                        reads=(pfx + "ctxT", pfx + "wpb"), writes=(pfx + "po",))
                P.op("vector", lambda E, fb=fb: E.tensor_tensor(
                    out=oev[:, fb * 512:(fb + 1) * 512], in0=po[:, :],
                    in1=bp_bc[:, fb * 512:(fb + 1) * 512], op=OP.add),
                    reads=(pfx + "po", pfx + "bp_bc"), writes=(pfx + "oev",))
            P.op("vector", lambda E, b=b: E.tensor_add(out=oev[:, :], in0=oev[:, :], in1=xr[b][:, :]),
                 reads=(pfx + "oev", f"{pfx}xr{b}"), writes=(pfx + "oev",))
            P.op("gpsimd", lambda E, ti=ti: E.dma_start(
                out=y[ti * 128:(ti + 1) * 128, :], in_=oev[:, :]),
                reads=(pfx + "oev",), writes=(ykey,))
    P.barrier()



def emit_stage_b(nc, P, es, x, xkey, y, ykey, gwT, wgT, wuT, wdT, sgT, suT, sdT,
                 identb, ones_f):
    HALF = 1536
    NH = HALF // 128
    lnh = nc.dram_tensor("b_lnT", [D, T], BF16, kind="Internal").ap()
    acc = nc.dram_tensor("b_acc", [T, D], F32, kind="Internal").ap()

    comb = es.enter_context(nc.sbuf_tensor("bcomb", [128, NT * 8], F32))

    with contextlib.ExitStack() as s0:
        gws = s0.enter_context(nc.sbuf_tensor("bgws", [128, NC_ * 8], F32))
        gwb = s0.enter_context(nc.sbuf_tensor("bgwb", [128, NC_ * 8], BF16))
        for c in range(NC_):
            P.op("sync", lambda E, c=c: E.dma_start(
                out=gws[:, c * 8:(c + 1) * 8], in_=gwT[c * 128:(c + 1) * 128, :]),
                writes=("bgws",))
        P.op("vector", lambda E: E.tensor_copy(out=gwb[:, :], in_=gws[:, :]),
             reads=("bgws",), writes=("bgwb",))
        xs = [s0.enter_context(nc.sbuf_tensor(f"bxs{i}", [128, 4 * D], F32)) for i in range(2)]
        lnb = s0.enter_context(nc.sbuf_tensor("blnb", [128, D], BF16))
        lnT = [s0.enter_context(nc.sbuf_tensor(f"blnT{i}", [128, NC_ * 512], BF16)) for i in range(2)]
        scr = mk_ln_scratch(s0, nc, "bln")
        psT = [s0.enter_context(nc.psum_tensor(f"bpsT{i}", [128, 128], BF16)) for i in range(2)]
        plog = s0.enter_context(nc.psum_tensor("bplog", [128, 8], F32))
        lg = s0.enter_context(nc.sbuf_tensor("blg", [128, 8], F32))
        m1 = s0.enter_context(nc.sbuf_tensor("bm1", [128, 1], F32))
        m2 = s0.enter_context(nc.sbuf_tensor("bm2", [128, 1], F32))
        is1 = s0.enter_context(nc.sbuf_tensor("bis1", [128, 8], F32))
        is2 = s0.enter_context(nc.sbuf_tensor("bis2", [128, 8], F32))
        msk2 = s0.enter_context(nc.sbuf_tensor("bmsk2", [128, 8], F32))
        dd = s0.enter_context(nc.sbuf_tensor("bdd", [128, 1], F32))
        w1 = s0.enter_context(nc.sbuf_tensor("bw1", [128, 1], F32))
        w2 = s0.enter_context(nc.sbuf_tensor("bw2", [128, 1], F32))
        c1 = s0.enter_context(nc.sbuf_tensor("bc1", [128, 8], F32))

        for st in range(NS):
            b = st % 2
            xk, lk = f"bxs{b}", f"blnT{b}"
            emit_ln_strip(P, x, xs[b], xk, lnT[b], lk, st, scr, "bln",
                          lnb, "blnb", identb, psT, ["bpsT0", "bpsT1"])
            for c in range(NC_):
                P.op("gpsimd", lambda E, st=st, c=c, b=b: E.dma_start(
                    out=lnh[c * 128:(c + 1) * 128, st * 512:(st + 1) * 512],
                    in_=lnT[b][:, c * 512:(c + 1) * 512]),
                    reads=(lk,), writes=("blnh",))
            for j in range(4):
                ti = st * 4 + j
                for ec in range(NC_):
                    P.op("tensor", lambda E, j=j, ec=ec, b=b: E.matmul(
                        out=plog[:, :], lhsT=lnT[b][:, ec * 512 + j * 128: ec * 512 + j * 128 + 128],
                        rhs=gwb[:, ec * 8:(ec + 1) * 8], start=(ec == 0), stop=(ec == NC_ - 1)),
                        reads=(lk, "bgwb"), writes=("bplog",))
                P.op("vector", lambda E: E.tensor_copy(out=lg[:, :], in_=plog[:, :]),
                     reads=("bplog",), writes=("blg",))
                P.op("vector", lambda E: E.reduce_max(out=m1[:, :], in_=lg[:, :], axis=AX.X),
                     reads=("blg",), writes=("bm1",))
                P.op("vector", lambda E: E.tensor_scalar(out=is1[:, :], in0=lg[:, :],
                     scalar1=m1[:, :], scalar2=None, op0=OP.is_ge),
                     reads=("blg", "bm1"), writes=("bis1",))
                P.op("vector", lambda E: E.scalar_tensor_tensor(out=msk2[:, :], in0=is1[:, :],
                     scalar=-1e30, in1=lg[:, :], op0=OP.mult, op1=OP.add),
                     reads=("bis1", "blg"), writes=("bmsk2",))
                P.op("vector", lambda E: E.reduce_max(out=m2[:, :], in_=msk2[:, :], axis=AX.X),
                     reads=("bmsk2",), writes=("bm2",))
                P.op("vector", lambda E: E.tensor_scalar(out=is2[:, :], in0=msk2[:, :],
                     scalar1=m2[:, :], scalar2=None, op0=OP.is_ge),
                     reads=("bmsk2", "bm2"), writes=("bis2",))
                P.op("vector", lambda E: E.tensor_tensor(out=dd[:, :], in0=m1[:, :], in1=m2[:, :], op=OP.subtract),
                     reads=("bm1", "bm2"), writes=("bdd",))
                P.op("scalar", lambda E: E.activation(out=w1[:, :], in_=dd[:, :], func=AF.Sigmoid),
                     reads=("bdd",), writes=("bw1",))
                P.op("vector", lambda E: E.tensor_scalar_mul(out=dd[:, :], in0=dd[:, :], scalar1=-1.0),
                     reads=("bdd",), writes=("bdd",))
                P.op("scalar", lambda E: E.activation(out=w2[:, :], in_=dd[:, :], func=AF.Sigmoid),
                     reads=("bdd",), writes=("bw2",))
                P.op("vector", lambda E: E.tensor_scalar_mul(out=c1[:, :], in0=is1[:, :], scalar1=w1[:, :]),
                     reads=("bis1", "bw1"), writes=("bc1",))
                P.op("vector", lambda E, ti=ti: E.scalar_tensor_tensor(
                    out=comb[:, ti * 8:(ti + 1) * 8], in0=is2[:, :], scalar=w2[:, :],
                    in1=c1[:, :], op0=OP.mult, op1=OP.add),
                    reads=("bis2", "bw2", "bc1"), writes=("bcomb",))
    P.barrier()

    with contextlib.ExitStack() as s1:
        wgb = s1.enter_context(nc.sbuf_tensor("bwgb", [128, NC_ * 1536], BF16))
        wub = s1.enter_context(nc.sbuf_tensor("bwub", [128, NC_ * 1536], BF16))
        wdb = s1.enter_context(nc.sbuf_tensor("bwdb", [128, NH * D], BF16))
        wst = s1.enter_context(nc.sbuf_tensor("bwst", [128, STG], F32))
        lns = [s1.enter_context(nc.sbuf_tensor(f"blns{i}", [128, NC_ * 512], BF16)) for i in range(2)]
        pg = s1.enter_context(nc.psum_tensor("bpg", [128, 512], F32))
        pu = s1.enter_context(nc.psum_tensor("bpu", [128, 512], F32))
        gs = s1.enter_context(nc.sbuf_tensor("bgs", [128, 512], F32))
        aT = s1.enter_context(nc.sbuf_tensor("baT", [128, NH * 512], BF16))
        py = s1.enter_context(nc.psum_tensor("bpy", [128, 512], F32))
        acs = [s1.enter_context(nc.sbuf_tensor(f"bacs{i}", [128, 4 * D], F32)) for i in range(2)]

        for e in [8] + list(range(8)):
            load_weightT(P, sgT if e == 8 else wgT[e], wgb, "bwgb", NC_, 1536, wst, "bwst")
            load_weightT(P, suT if e == 8 else wuT[e], wub, "bwub", NC_, 1536, wst, "bwst")
            load_weightT(P, sdT if e == 8 else wdT[e], wdb, "bwdb", NH, D, wst, "bwst")
            for st in range(NS):
                b = st % 2
                lk = f"blns{b}"
                for c in range(NC_):
                    P.op("sync", lambda E, st=st, c=c, b=b: E.dma_start(
                        out=lns[b][:, c * 512:(c + 1) * 512],
                        in_=lnh[c * 128:(c + 1) * 128, st * 512:(st + 1) * 512]),
                        reads=("blnh",), writes=(lk,))
                if e != 8:
                    for j in range(4):
                        P.op("sync", lambda E, st=st, j=j, b=b: E.dma_start(
                            out=acs[b][:, j * D:(j + 1) * D],
                            in_=acc[st * 512 + j * 128: st * 512 + (j + 1) * 128, :]),
                            reads=("bacc",), writes=(f"bacs{b}",))
                for hc in range(NH):
                    for (ps, wb, wkey, pk) in ((pg, wgb, "bwgb", "bpg"), (pu, wub, "bwub", "bpu")):
                        for ec in range(NC_):
                            P.op("tensor", lambda E, hc=hc, ec=ec, ps=ps, wb=wb, b=b: E.matmul(
                                out=ps[:, :], lhsT=wb[:, ec * 1536 + hc * 128: ec * 1536 + hc * 128 + 128],
                                rhs=lns[b][:, ec * 512:(ec + 1) * 512],
                                start=(ec == 0), stop=(ec == NC_ - 1)),
                                reads=(wkey, lk), writes=(pk,))
                    P.op("scalar", lambda E: E.activation(out=gs[:, :], in_=pg[:, :], func=AF.Silu),
                         reads=("bpg",), writes=("bgs",))
                    P.op("vector", lambda E, hc=hc: E.tensor_tensor(
                        out=aT[:, hc * 512:(hc + 1) * 512], in0=gs[:, :], in1=pu[:, :], op=OP.mult),
                        reads=("bgs", "bpu"), writes=("baT",))
                for j in range(4):
                    ti = st * 4 + j
                    for fb in range(2):
                        for hc in range(NH):
                            P.op("tensor", lambda E, j=j, fb=fb, hc=hc: E.matmul(
                                out=py[:, :], lhsT=aT[:, hc * 512 + j * 128: hc * 512 + j * 128 + 128],
                                rhs=wdb[:, hc * D + fb * 512: hc * D + fb * 512 + 512],
                                start=(hc == 0), stop=(hc == NH - 1)),
                                reads=("baT", "bwdb"), writes=("bpy",))
                        sl = (j * D + fb * 512, j * D + fb * 512 + 512)
                        if e == 8:
                            P.op("vector", lambda E, sl=sl, b=b: E.tensor_copy(
                                out=acs[b][:, sl[0]:sl[1]], in_=py[:, :]),
                                reads=("bpy",), writes=(f"bacs{b}",))
                        else:
                            P.op("vector", lambda E, sl=sl, ti=ti, e=e, b=b: E.scalar_tensor_tensor(
                                out=acs[b][:, sl[0]:sl[1]], in0=py[:, :],
                                scalar=comb[:, ti * 8 + e: ti * 8 + e + 1],
                                in1=acs[b][:, sl[0]:sl[1]], op0=OP.mult, op1=OP.add),
                                reads=("bpy", "bcomb", f"bacs{b}"), writes=(f"bacs{b}",))
                for j in range(4):
                    P.op("gpsimd", lambda E, st=st, j=j, b=b: E.dma_start(
                        out=acc[st * 512 + j * 128: st * 512 + (j + 1) * 128, :],
                        in_=acs[b][:, j * D:(j + 1) * D]),
                        reads=(f"bacs{b}",), writes=("bacc",))
    P.barrier()

    with contextlib.ExitStack() as s2:
        xs = [s2.enter_context(nc.sbuf_tensor(f"bfx{i}", [128, D], F32)) for i in range(2)]
        ac = [s2.enter_context(nc.sbuf_tensor(f"bfa{i}", [128, D], F32)) for i in range(2)]
        for ti in range(NT):
            b = ti % 2
            P.op("sync", lambda E, ti=ti, b=b: E.dma_start(
                out=xs[b][:, :], in_=x[ti * 128:(ti + 1) * 128, :]),
                reads=(xkey,), writes=(f"bfx{b}",))
            P.op("sync", lambda E, ti=ti, b=b: E.dma_start(
                out=ac[b][:, :], in_=acc[ti * 128:(ti + 1) * 128, :]),
                reads=("bacc",), writes=(f"bfa{b}",))
            P.op("vector", lambda E, b=b: E.tensor_add(out=ac[b][:, :], in0=ac[b][:, :], in1=xs[b][:, :]),
                 reads=(f"bfa{b}", f"bfx{b}"), writes=(f"bfa{b}",))
            P.op("gpsimd", lambda E, ti=ti, b=b: E.dma_start(
                out=y[ti * 128:(ti + 1) * 128, :], in_=ac[b][:, :]),
                reads=(f"bfa{b}",), writes=(ykey,))
    P.barrier()



C_PARTS = "123"
C_HALVES = "fp"


def emit_stage_c(nc, P, es, x, xkey, y, ykey, wq, wk, wv, bq_pc, bk_pc, bv, maskc,
                 identb, identf, ones_f):
    qfm = nc.dram_tensor("c_qfm", [D, T], BF16, kind="Internal").ap()
    kfm = nc.dram_tensor("c_kfm", [D, T], BF16, kind="Internal").ap()
    vtm = nc.dram_tensor("c_vtm", [T, D], BF16, kind="Internal").ap()
    SC = 0.125
    hk = "chbm"

    P.mute = "1" not in C_PARTS
    with contextlib.ExitStack() as s1:
        wst = s1.enter_context(nc.sbuf_tensor("cwst", [128, STG], F32))
        wqb = s1.enter_context(nc.sbuf_tensor("cwqb", [128, NC_ * D], BF16))
        wkb = s1.enter_context(nc.sbuf_tensor("cwkb", [128, NC_ * D], BF16))
        wvb = s1.enter_context(nc.sbuf_tensor("cwvb", [128, NC_ * D], BF16))
        load_weightT(P, wq, wqb, "cwqb", NC_, D, wst, "cwst")
        load_weightT(P, wk, wkb, "cwkb", NC_, D, wst, "cwst")
        load_weightT(P, wv, wvb, "cwvb", NC_, D, wst, "cwst")
        bqs = s1.enter_context(nc.sbuf_tensor("cbqs", [128, NC_], F32))
        bks = s1.enter_context(nc.sbuf_tensor("cbks", [128, NC_], F32))
        P.op("sync", lambda E: E.dma_start(out=bqs[:, :], in_=bq_pc[:, :]), writes=("cbqs",))
        P.op("sync", lambda E: E.dma_start(out=bks[:, :], in_=bk_pc[:, :]), writes=("cbks",))
        bv_bc = bias_bcast(P, s1, nc, bv, "cbv", D, ones_f)

        xs = [s1.enter_context(nc.sbuf_tensor(f"cxs{i}", [128, 4 * D], F32)) for i in range(2)]
        lnb = s1.enter_context(nc.sbuf_tensor("clnb", [128, D], BF16))
        lnT = [s1.enter_context(nc.sbuf_tensor(f"clnT{i}", [128, NC_ * 512], BF16)) for i in range(2)]
        scr = mk_ln_scratch(s1, nc, "cln")
        psT = [s1.enter_context(nc.psum_tensor(f"cpsT{i}", [128, 128], BF16)) for i in range(2)]
        pq = s1.enter_context(nc.psum_tensor("cpq", [128, 512], F32))
        pv = s1.enter_context(nc.psum_tensor("cpv", [128, 512], F32))
        ev = [s1.enter_context(nc.sbuf_tensor(f"cev{i}", [128, 512], BF16)) for i in range(2)]

        for st in range(NS):
            b = st % 2
            xk, lk = f"cxs{b}", f"clnT{b}"
            emit_ln_strip(P, x, xs[b], xk, lnT[b], lk, st, scr, "cln",
                          lnb, "clnb", identb, psT, ["cpsT0", "cpsT1"])
            for (wb, wkey, bcol, bkey, dst) in ((wqb, "cwqb", bqs, "cbqs", qfm),
                                                (wkb, "cwkb", bks, "cbks", kfm)):
                for fc in range(NC_):
                    for ec in range(NC_):
                        P.op("tensor", lambda E, fc=fc, ec=ec, wb=wb, b=b: E.matmul(
                            out=pq[:, :], lhsT=wb[:, ec * D + fc * 128: ec * D + fc * 128 + 128],
                            rhs=lnT[b][:, ec * 512:(ec + 1) * 512],
                            start=(ec == 0), stop=(ec == NC_ - 1)),
                            reads=(wkey, lk), writes=("cpq",))
                    eb = fc % 2
                    P.op("vector", lambda E, fc=fc, eb=eb, bcol=bcol, bkey=bkey: E.tensor_scalar_add(
                        out=ev[eb][:, :], in0=pq[:, :], scalar1=bcol[:, fc:fc + 1]),
                        reads=("cpq", bkey), writes=(f"cev{eb}",))
                    P.op("gpsimd", lambda E, fc=fc, eb=eb, st=st, dst=dst: E.dma_start(
                        out=dst[fc * 128:(fc + 1) * 128, st * 512:(st + 1) * 512], in_=ev[eb][:, :]),
                        reads=(f"cev{eb}",), writes=(hk,))
            for tj in range(4):
                for fb in range(2):
                    for ec in range(NC_):
                        P.op("tensor", lambda E, tj=tj, fb=fb, ec=ec, b=b: E.matmul(
                            out=pv[:, :], lhsT=lnT[b][:, ec * 512 + tj * 128: ec * 512 + tj * 128 + 128],
                            rhs=wvb[:, ec * D + fb * 512: ec * D + fb * 512 + 512],
                            start=(ec == 0), stop=(ec == NC_ - 1)),
                            reads=("cwvb", lk), writes=("cpv",))
                    eb = (tj * 2 + fb) % 2
                    P.op("vector", lambda E, fb=fb, eb=eb: E.tensor_tensor(
                        out=ev[eb][:, :], in0=pv[:, :], in1=bv_bc[:, fb * 512:(fb + 1) * 512], op=OP.add),
                        reads=("cpv", "cbv_bc"), writes=(f"cev{eb}",))
                    P.op("gpsimd", lambda E, st=st, tj=tj, fb=fb, eb=eb: E.dma_start(
                        out=vtm[st * 512 + tj * 128: st * 512 + (tj + 1) * 128, fb * 512:(fb + 1) * 512],
                        in_=ev[eb][:, :]),
                        reads=(f"cev{eb}",), writes=(hk,))
    P.barrier()

    P.mute = False
    P.mute = "2" not in C_PARTS
    with contextlib.ExitStack() as s2:
        msk = s2.enter_context(nc.sbuf_tensor("cmsk", [128, 384], F32))
        P.op("sync", lambda E: E.dma_start(out=msk[:, :], in_=maskc[:, :]), writes=("cmsk",))
        qf = [s2.enter_context(nc.sbuf_tensor(f"cqf{i}", [128, NC_ * 128], BF16)) for i in range(2)]
        kb_ = [s2.enter_context(nc.sbuf_tensor(f"ckb{i}", [128, NC_ * 256], BF16)) for i in range(2)]
        kg = [s2.enter_context(nc.sbuf_tensor(f"ckg{i}", [128, NC_ * 128], BF16)) for i in range(2)]
        vb_ = [s2.enter_context(nc.sbuf_tensor(f"cvb{i}", [128, 2 * D], BF16)) for i in range(2)]
        vg = [s2.enter_context(nc.sbuf_tensor(f"cvg{i}", [128, D], BF16)) for i in range(2)]
        xr = [s2.enter_context(nc.sbuf_tensor(f"cxr{i}", [128, D], F32)) for i in range(2)]
        pen = s2.enter_context(nc.psum_tensor("cpen", [128, 384], F32))
        ens = s2.enter_context(nc.sbuf_tensor("cens", [128, 384], F32))
        mx = s2.enter_context(nc.sbuf_tensor("cmx", [128, 1], F32))
        den = s2.enter_context(nc.sbuf_tensor("cden", [128, 1], F32))
        attb = s2.enter_context(nc.sbuf_tensor("cattb", [128, 384], BF16))
        attT = s2.enter_context(nc.sbuf_tensor("cattT", [128, 3 * 128], BF16))
        psU = [s2.enter_context(nc.psum_tensor(f"cpsU{i}", [128, 128], BF16)) for i in range(2)]
        pctx = s2.enter_context(nc.psum_tensor("cpctx", [64, 128], F32))
        ctxT = s2.enter_context(nc.sbuf_tensor("cctxT", [128, NC_ * 128], F32))
        psO = [s2.enter_context(nc.psum_tensor(f"cpsO{i}", [128, 128], F32)) for i in range(2)]
        oev = s2.enter_context(nc.sbuf_tensor("coev", [128, D], F32))

        for ti in range(1, NT - 1):
            b = ti % 2
            base = ti * 128 - 64
            for c in range(NC_):
                P.op("sync", lambda E, ti=ti, c=c, b=b: E.dma_start(
                    out=qf[b][:, c * 128:(c + 1) * 128],
                    in_=qfm[c * 128:(c + 1) * 128, ti * 128:(ti + 1) * 128]),
                    reads=(hk,), writes=(f"cqf{b}",))
                P.op("sync", lambda E, base=base, c=c, b=b: E.dma_start(
                    out=kb_[b][:, c * 256:(c + 1) * 256],
                    in_=kfm[c * 128:(c + 1) * 128, base: base + 256]),
                    reads=(hk,), writes=(f"ckb{b}",))
                P.op("sync", lambda E, c=c, b=b: E.dma_start(
                    out=kg[b][:, c * 128: c * 128 + 64],
                    in_=kfm[c * 128:(c + 1) * 128, 0:64]),
                    reads=(hk,), writes=(f"ckg{b}",))
                P.op("sync", lambda E, c=c, b=b: E.dma_start(
                    out=kg[b][:, c * 128 + 64: c * 128 + 128],
                    in_=kfm[c * 128:(c + 1) * 128, T - 64: T]),
                    reads=(hk,), writes=(f"ckg{b}",))
            for j2 in range(2):
                P.op("sync", lambda E, base=base, j2=j2, b=b: E.dma_start(
                    out=vb_[b][:, j2 * D:(j2 + 1) * D],
                    in_=vtm[base + j2 * 128: base + (j2 + 1) * 128, :]),
                    reads=(hk,), writes=(f"cvb{b}",))
            P.op("sync", lambda E, b=b: E.dma_start(
                out=vg[b][0:64, :], in_=vtm[0:64, :]),
                reads=(hk,), writes=(f"cvg{b}",))
            P.op("sync", lambda E, b=b: E.dma_start(
                out=vg[b][64:128, :], in_=vtm[T - 64: T, :]),
                reads=(hk,), writes=(f"cvg{b}",))
            P.op("sync", lambda E, ti=ti, b=b: E.dma_start(
                out=xr[b][:, :], in_=x[ti * 128:(ti + 1) * 128, :]),
                reads=(xkey,), writes=(f"cxr{b}",))
            for h in range(16):
                c, o = h // 2, (h % 2) * 64
                P.op("tensor", lambda E, c=c, o=o, b=b: E.matmul(
                    out=pen[:, 0:256], lhsT=qf[b][o:o + 64, c * 128:(c + 1) * 128],
                    rhs=kb_[b][o:o + 64, c * 256:(c + 1) * 256], start=True, stop=True),
                    reads=(f"cqf{b}", f"ckb{b}"), writes=("cpen",))
                P.op("tensor", lambda E, c=c, o=o, b=b: E.matmul(
                    out=pen[:, 256:384], lhsT=qf[b][o:o + 64, c * 128:(c + 1) * 128],
                    rhs=kg[b][o:o + 64, c * 128:(c + 1) * 128], start=True, stop=True),
                    reads=(f"cqf{b}", f"ckg{b}"), writes=("cpen",))
                P.op("vector", lambda E: E.tensor_tensor(
                    out=ens[:, :], in0=pen[:, :], in1=msk[:, :], op=OP.add),
                    reads=("cpen", "cmsk"), writes=("cens",))
                emit_softmax(P, "c", ens, "cens", mx, den, attb, "cattb", SC)
                for k3 in range(3):
                    i = k3 % 2
                    P.op("tensor", lambda E, k3=k3, i=i: E.transpose(
                        out=psU[i][:, :], in_=attb[:, k3 * 128:(k3 + 1) * 128], identity=identb[:, :]),
                        reads=("cattb", "identb"), writes=(f"cpsU{i}",))
                    P.op("vector", lambda E, k3=k3, i=i: E.tensor_copy(
                        out=attT[:, k3 * 128:(k3 + 1) * 128], in_=psU[i][:, :]),
                        reads=(f"cpsU{i}",), writes=("cattT",))
                for k2 in range(2):
                    P.op("tensor", lambda E, k2=k2, h=h, b=b: E.matmul(
                        out=pctx[:, :], lhsT=vb_[b][:, k2 * D + h * 64: k2 * D + h * 64 + 64],
                        rhs=attT[:, k2 * 128:(k2 + 1) * 128], start=(k2 == 0), stop=False),
                        reads=(f"cvb{b}", "cattT"), writes=("cpctx",))
                P.op("tensor", lambda E, h=h, b=b: E.matmul(
                    out=pctx[:, :], lhsT=vg[b][:, h * 64: h * 64 + 64],
                    rhs=attT[:, 256:384], start=False, stop=True),
                    reads=(f"cvg{b}", "cattT"), writes=("cpctx",))
                P.op("vector", lambda E, c=c, o=o: E.tensor_copy(
                    out=ctxT[o:o + 64, c * 128:(c + 1) * 128], in_=pctx[:, :]),
                    reads=("cpctx",), writes=("cctxT",))
            for c in range(NC_):
                i = c % 2
                P.op("tensor", lambda E, c=c, i=i: E.transpose(
                    out=psO[i][:, :], in_=ctxT[:, c * 128:(c + 1) * 128], identity=identf[:, :]),
                    reads=("cctxT", "identf"), writes=(f"cpsO{i}",))
                P.op("vector", lambda E, c=c, i=i, b=b: E.tensor_tensor(
                    out=oev[:, c * 128:(c + 1) * 128], in0=psO[i][:, :],
                    in1=xr[b][:, c * 128:(c + 1) * 128], op=OP.add),
                    reads=(f"cpsO{i}", f"cxr{b}"), writes=("coev",))
            P.op("gpsimd", lambda E, ti=ti: E.dma_start(
                out=y[ti * 128:(ti + 1) * 128, :], in_=oev[:, :]),
                reads=("coev",), writes=(ykey,))
    P.barrier()

    P.mute = False
    P.mute = "3" not in C_PARTS
    with contextlib.ExitStack() as s3:
        kall = s3.enter_context(nc.sbuf_tensor("ckall", [128, NC_ * T], BF16))
        vall = s3.enter_context(nc.sbuf_tensor("cvall", [128, NT * D], BF16))
        for c in range(NC_):
            P.op("sync", lambda E, c=c: E.dma_start(
                out=kall[:, c * T:(c + 1) * T], in_=kfm[c * 128:(c + 1) * 128, :]),
                reads=(hk,), writes=("ckall",))
        for j in range(NT):
            P.op("sync", lambda E, j=j: E.dma_start(
                out=vall[:, j * D:(j + 1) * D], in_=vtm[j * 128:(j + 1) * 128, :]),
                reads=(hk,), writes=("cvall",))
        qfe = s3.enter_context(nc.sbuf_tensor("ceqf", [128, NC_ * 128], BF16))
        xre = s3.enter_context(nc.sbuf_tensor("cexr", [128, D], F32))
        pE = s3.enter_context(nc.psum_tensor("cpE", [128, 512], F32))
        enb = s3.enter_context(nc.sbuf_tensor("cenb", [128, T], F32))
        mxe = s3.enter_context(nc.sbuf_tensor("cemx", [128, 1], F32))
        dene = s3.enter_context(nc.sbuf_tensor("ceden", [128, 1], F32))
        attbe = s3.enter_context(nc.sbuf_tensor("ceattb", [128, T], BF16))
        attTe = s3.enter_context(nc.sbuf_tensor("ceattT", [128, T], BF16))
        psE = [s3.enter_context(nc.psum_tensor(f"cpsE{i}", [128, 128], BF16)) for i in range(2)]
        pctxe = s3.enter_context(nc.psum_tensor("cepctx", [64, 128], F32))
        ctxTe = s3.enter_context(nc.sbuf_tensor("cectxT", [128, NC_ * 128], F32))
        psOe = [s3.enter_context(nc.psum_tensor(f"cpsF{i}", [128, 128], F32)) for i in range(2)]
        oeve = s3.enter_context(nc.sbuf_tensor("ceoev", [128, D], F32))

        for ei, ti in enumerate((0, NT - 1)):
            for c in range(NC_):
                P.op("sync", lambda E, ti=ti, c=c: E.dma_start(
                    out=qfe[:, c * 128:(c + 1) * 128],
                    in_=qfm[c * 128:(c + 1) * 128, ti * 128:(ti + 1) * 128]),
                    reads=(hk,), writes=("ceqf",))
            P.op("sync", lambda E, ti=ti: E.dma_start(
                out=xre[:, :], in_=x[ti * 128:(ti + 1) * 128, :]),
                reads=(xkey,), writes=("cexr",))
            for h in range(16):
                c, o = h // 2, (h % 2) * 64
                for ks in range(NC_):
                    P.op("tensor", lambda E, c=c, o=o, ks=ks: E.matmul(
                        out=pE[:, :], lhsT=qfe[o:o + 64, c * 128:(c + 1) * 128],
                        rhs=kall[o:o + 64, c * T + ks * 512: c * T + (ks + 1) * 512],
                        start=True, stop=True),
                        reads=("ceqf", "ckall"), writes=("cpE",))
                    P.op("vector", lambda E, ks=ks: E.tensor_copy(
                        out=enb[:, ks * 512:(ks + 1) * 512], in_=pE[:, :]),
                        reads=("cpE",), writes=("cenb",))
                if ti == 0:
                    P.op("vector", lambda E: E.memset(enb[64:128, 192:4032], -1e30),
                         reads=(), writes=("cenb",))
                else:
                    P.op("vector", lambda E: E.memset(enb[0:64, 64:3904], -1e30),
                         reads=(), writes=("cenb",))
                emit_softmax(P, "ce", enb, "cenb", mxe, dene, attbe, "ceattb", SC)
                for kc in range(NT):
                    i = kc % 2
                    P.op("tensor", lambda E, kc=kc, i=i: E.transpose(
                        out=psE[i][:, :], in_=attbe[:, kc * 128:(kc + 1) * 128],
                        identity=identb[:, :]),
                        reads=("ceattb", "identb"), writes=(f"cpsE{i}",))
                    P.op("vector", lambda E, kc=kc, i=i: E.tensor_copy(
                        out=attTe[:, kc * 128:(kc + 1) * 128], in_=psE[i][:, :]),
                        reads=(f"cpsE{i}",), writes=("cattTe",))
                for kc in range(NT):
                    P.op("tensor", lambda E, kc=kc, h=h: E.matmul(
                        out=pctxe[:, :], lhsT=vall[:, kc * D + h * 64: kc * D + h * 64 + 64],
                        rhs=attTe[:, kc * 128:(kc + 1) * 128],
                        start=(kc == 0), stop=(kc == NT - 1)),
                        reads=("cvall", "cattTe"), writes=("cepctx",))
                P.op("vector", lambda E, c=c, o=o: E.tensor_copy(
                    out=ctxTe[o:o + 64, c * 128:(c + 1) * 128], in_=pctxe[:, :]),
                    reads=("cepctx",), writes=("cectxT",))
            for c in range(NC_):
                i = c % 2
                P.op("tensor", lambda E, c=c, i=i: E.transpose(
                    out=psOe[i][:, :], in_=ctxTe[:, c * 128:(c + 1) * 128], identity=identf[:, :]),
                    reads=("cectxT", "identf"), writes=(f"cpsF{i}",))
                P.op("vector", lambda E, c=c, i=i: E.tensor_tensor(
                    out=oeve[:, c * 128:(c + 1) * 128], in0=psOe[i][:, :],
                    in1=xre[:, c * 128:(c + 1) * 128], op=OP.add),
                    reads=(f"cpsF{i}", "cexr"), writes=("ceoev",))
            P.op("gpsimd", lambda E, ti=ti: E.dma_start(
                out=y[ti * 128:(ti + 1) * 128, :], in_=oeve[:, :]),
                reads=("ceoev",), writes=(ykey,))
    P.mute = False
    P.barrier()



def emit_stage_d(nc, P, es, x, xkey, y, ykey, wT, alpha, bias, identb, ones_f):
    wbf = es.enter_context(nc.sbuf_tensor("dwbf", [128, NC_ * D], BF16))
    bias_bc = bias_bcast(P, es, nc, bias, "dbb", D, ones_f)
    with contextlib.ExitStack() as s0:
        wst = s0.enter_context(nc.sbuf_tensor("dwst", [128, NC_ * D], F32))
        t1 = s0.enter_context(nc.sbuf_tensor("dt1", [128, NC_ * D], F32))
        t2 = s0.enter_context(nc.sbuf_tensor("dt2", [128, NC_ * D], F32))
        rs = s0.enter_context(nc.sbuf_tensor("drs", [128, 1], F32))
        ptot = s0.enter_context(nc.psum_tensor("dptot", [1, 1], F32))
        dsc = s0.enter_context(nc.sbuf_tensor("ddsc", [1, 1], F32))
        pcol = s0.enter_context(nc.psum_tensor("dpcol", [128, 1], F32))
        dcol = s0.enter_context(nc.sbuf_tensor("ddcol", [128, 1], F32))
        ncol = s0.enter_context(nc.sbuf_tensor("dncol", [128, 1], F32))
        asb = s0.enter_context(nc.sbuf_tensor("dasb", [1, 1], F32))
        acol = s0.enter_context(nc.sbuf_tensor("dacol", [128, 1], F32))

        for c in range(NC_):
            P.op("sync", lambda E, c=c: E.dma_start(
                out=wst[:, c * D:(c + 1) * D], in_=wT[c * 128:(c + 1) * 128, :]),
                writes=("dwst",))
        P.op("sync", lambda E: E.dma_start(out=asb[:, :], in_=alpha[None, :]), writes=("dasb",))
        P.op("vector", lambda E: E.reduce_sum(out=rs[:, :], in_=wst[:, :], axis=AX.X,
             apply_absolute_value=True),
             reads=("dwst",), writes=("drs",))
        P.op("tensor", lambda E: E.matmul(out=ptot[:, :], lhsT=ones_f[0:128, 0:1], rhs=rs[:, :],
             start=True, stop=True),
             reads=("drs", "ones"), writes=("dptot",))
        P.op("scalar", lambda E: E.mul(out=dsc[:, :], in_=ptot[:, :], mul=0.7 / (D * D)),
             reads=("dptot",), writes=("ddsc",))
        P.op("tensor", lambda E: E.matmul(out=pcol[:, :], lhsT=ones_f[0:1, 0:128], rhs=dsc[:, :],
             start=True, stop=True),
             reads=("ddsc", "ones"), writes=("dpcol",))
        P.op("vector", lambda E: E.tensor_copy(out=dcol[:, :], in_=pcol[:, :]),
             reads=("dpcol",), writes=("ddcol",))
        P.op("vector", lambda E: E.tensor_scalar_mul(out=ncol[:, :], in0=dcol[:, :], scalar1=-1.0),
             reads=("ddcol",), writes=("dncol",))
        P.op("tensor", lambda E: E.matmul(out=pcol[:, :], lhsT=ones_f[0:1, 0:128], rhs=asb[:, :],
             start=True, stop=True),
             reads=("dasb", "ones"), writes=("dpcol",))
        P.op("vector", lambda E: E.tensor_copy(out=acol[:, :], in_=pcol[:, :]),
             reads=("dpcol",), writes=("dacol",))
        P.op("vector", lambda E: E.tensor_scalar(out=t1[:, :], in0=wst[:, :], scalar1=dcol[:, :],
             scalar2=None, op0=OP.is_gt),
             reads=("dwst", "ddcol"), writes=("dt1",))
        P.op("vector", lambda E: E.tensor_scalar(out=t2[:, :], in0=wst[:, :], scalar1=ncol[:, :],
             scalar2=None, op0=OP.is_lt),
             reads=("dwst", "dncol"), writes=("dt2",))
        P.op("vector", lambda E: E.tensor_sub(out=t1[:, :], in0=t1[:, :], in1=t2[:, :]),
             reads=("dt1", "dt2"), writes=("dt1",))
        P.op("vector", lambda E: E.tensor_scalar_mul(out=wbf[:, :], in0=t1[:, :], scalar1=acol[:, :]),
             reads=("dt1", "dacol"), writes=("dwbf",))
    P.barrier()

    with contextlib.ExitStack() as s1:
        xs = [s1.enter_context(nc.sbuf_tensor(f"dxs{i}", [128, 4 * D], F32)) for i in range(2)]
        lnb = s1.enter_context(nc.sbuf_tensor("dlnb", [128, D], BF16))
        lnT = [s1.enter_context(nc.sbuf_tensor(f"dlnT{i}", [128, NC_ * 512], BF16)) for i in range(2)]
        scr = mk_ln_scratch(s1, nc, "dln")
        psT = [s1.enter_context(nc.psum_tensor(f"dpsT{i}", [128, 128], BF16)) for i in range(2)]
        po = s1.enter_context(nc.psum_tensor("dpo", [128, 512], F32))
        oev = [s1.enter_context(nc.sbuf_tensor(f"doev{i}", [128, D], F32)) for i in range(2)]

        for st in range(NS):
            b = st % 2
            xk, lk = f"dxs{b}", f"dlnT{b}"
            emit_ln_strip(P, x, xs[b], xk, lnT[b], lk, st, scr, "dln",
                          lnb, "dlnb", identb, psT, ["dpsT0", "dpsT1"])
            for j in range(4):
                ob = j % 2
                for fb in range(2):
                    for ec in range(NC_):
                        P.op("tensor", lambda E, j=j, fb=fb, ec=ec, b=b: E.matmul(
                            out=po[:, :], lhsT=lnT[b][:, ec * 512 + j * 128: ec * 512 + j * 128 + 128],
                            rhs=wbf[:, ec * D + fb * 512: ec * D + fb * 512 + 512],
                            start=(ec == 0), stop=(ec == NC_ - 1)),
                            reads=(lk, "dwbf"), writes=("dpo",))
                    P.op("vector", lambda E, fb=fb, ob=ob: E.tensor_tensor(
                        out=oev[ob][:, fb * 512:(fb + 1) * 512], in0=po[:, :],
                        in1=bias_bc[:, fb * 512:(fb + 1) * 512], op=OP.add),
                        reads=("dpo", "dbb_bc"), writes=(f"doev{ob}",))
                P.op("vector", lambda E, j=j, ob=ob, b=b: E.tensor_add(
                    out=oev[ob][:, :], in0=oev[ob][:, :], in1=xs[b][:, j * D:(j + 1) * D]),
                    reads=(f"doev{ob}", xk), writes=(f"doev{ob}",))
                P.op("gpsimd", lambda E, st=st, j=j, ob=ob: E.dma_start(
                    out=y[st * 512 + j * 128: st * 512 + (j + 1) * 128, :], in_=oev[ob][:, :]),
                    reads=(f"doev{ob}",), writes=(ykey,))
    P.barrier()



def _consts():
    identb = np.eye(128, dtype=ml_dtypes.bfloat16)
    identf = np.eye(128, dtype=np.float32)
    ones = np.ones((128, 128), dtype=np.float32)
    i = np.arange(128)[:, None]
    j = np.arange(384)[None, :]
    mA = np.concatenate(
        [np.where(np.abs(v * 128 + i - j) <= 128, 0.0, -1e30).astype(np.float32)
         for v in range(3)], axis=1)
    mC = np.zeros((128, 384), dtype=np.float32)
    jb = np.arange(256)[None, :] // 64
    lo = (np.arange(128)[:, None] >= 64).astype(np.int64)
    valid = (jb >= lo) & (jb <= lo + 2)
    mC[:, 0:256] = np.where(valid, 0.0, np.float32(-1e30))
    col = np.arange(4096)[None, :]
    row = np.arange(128)[:, None]
    mE0 = np.where((row >= 64) & (col >= 192) & (col < 4032), np.float32(-1e30), 0.0).astype(np.float32)
    mE1 = np.where((row < 64) & (col >= 64) & (col < 3904), np.float32(-1e30), 0.0).astype(np.float32)
    mE = np.concatenate([mE0, mE1], axis=1)
    return identb, identf, ones, mA, mC, mE


def build_program(stages):
    nc = bass.Bass("TRN2", target_bir_lowering=False, debug=False)
    P = Prog(nc)
    din = lambda n, s: nc.dram_tensor(n, s, F32, kind="ExternalInput").ap()

    identb_t = nc.dram_tensor("identb", [128, 128], BF16, kind="ExternalInput").ap()
    identf_t = din("identf", [128, 128])
    ones_t = din("ones", [128, 128])

    x0 = din("x0", [T, D])
    yout = nc.dram_tensor("yout", [T, D], F32, kind="ExternalOutput").ap()

    with contextlib.ExitStack() as es:
        identb = es.enter_context(nc.sbuf_tensor("sidentb", [128, 128], BF16))
        identf = es.enter_context(nc.sbuf_tensor("sidentf", [128, 128], F32))
        ones_f = es.enter_context(nc.sbuf_tensor("sones", [128, 128], F32))
        P.op("sync", lambda E: E.dma_start(out=identb[:, :], in_=identb_t[:, :]), writes=("identb",))
        P.op("sync", lambda E: E.dma_start(out=identf[:, :], in_=identf_t[:, :]), writes=("identf",))
        P.op("sync", lambda E: E.dma_start(out=ones_f[:, :], in_=ones_t[:, :]), writes=("ones",))

        cur, ck = x0, "x_in"
        for si, stg in enumerate(stages):
            last = si == len(stages) - 1
            if last:
                nxt, nk = yout, "y_out"
            else:
                nxt = nc.dram_tensor(f"mid{si}", [T, D], F32, kind="Internal").ap()
                nk = f"mid{si}"
            if stg == "a":
                emit_stage_a(nc, P, es, cur, ck, nxt, nk,
                             din("a_wq", [D, D]), din("a_wk", [D, D]), din("a_wv", [D, D]),
                             din("a_wp", [D, D]), din("a_bq", [128, NC_]), din("a_bk", [128, NC_]),
                             din("a_bv", [D]), din("a_bp", [D]), din("a_masks", [128, 3 * 384]),
                             identb, ones_f)
            elif stg == "b":
                emit_stage_b(nc, P, es, cur, ck, nxt, nk,
                             din("b_gwT", [D, 8]), din("b_wgT", [8, D, 1536]), din("b_wuT", [8, D, 1536]),
                             din("b_wdT", [8, 1536, D]), din("b_sgT", [D, 1536]), din("b_suT", [D, 1536]),
                             din("b_sdT", [1536, D]), identb, ones_f)
            elif stg == "c":
                emit_stage_c(nc, P, es, cur, ck, nxt, nk,
                             din("c_wq", [D, D]), din("c_wk", [D, D]), din("c_wv", [D, D]),
                             din("c_bq", [128, NC_]), din("c_bk", [128, NC_]), din("c_bv", [D]),
                             din("c_maskc", [128, 384]), identb, identf, ones_f)
            elif stg == "d":
                emit_stage_d(nc, P, es, cur, ck, nxt, nk,
                             din("d_wT", [D, D]), din("d_alpha", [1]), din("d_bias", [D]),
                             identb, ones_f)
            cur, ck = nxt, nk
            if not last:
                P.epoch_break()

        P.barrier()
        P.sems = {(ep, e): es.enter_context(nc.semaphore(f"sem{ep}_{e}"))
                  for ep in range(P.epoch + 1) for e in ENGS}
        with nc.Block() as block:
            P.replay(block)
    return nc


def _pc(b):
    return np.ascontiguousarray(np.asarray(b, dtype=np.float32).reshape(NC_, 128).T)


def _tr(w):
    return np.ascontiguousarray(np.asarray(w, dtype=np.float32).T)


def stage_input_dicts(inp):
    identb, identf, ones, mA, mC, mE = _consts()
    base = {"identb": identb, "identf": identf, "ones": ones}
    si = {
        "a": dict(a_wq=_tr(inp["L0_attn_Wq"]), a_wk=_tr(inp["L0_attn_Wk"]),
                  a_wv=_tr(inp["L0_attn_Wv"]), a_wp=_tr(inp["L0_attn_Wp"]),
                  a_bq=_pc(inp["L0_attn_bq"]), a_bk=_pc(inp["L0_attn_bk"]),
                  a_bv=np.ascontiguousarray(inp["L0_attn_bv"], np.float32),
                  a_bp=np.ascontiguousarray(inp["L0_attn_bp"], np.float32),
                  a_masks=mA),
        "b": dict(b_gwT=_tr(inp["L0_ffn_gate_w"]),
                  b_wgT=np.ascontiguousarray(np.transpose(
                      np.asarray(inp["L0_ffn_w_up"], np.float32)[:, :1536, :], (0, 2, 1))),
                  b_wuT=np.ascontiguousarray(np.transpose(
                      np.asarray(inp["L0_ffn_w_up"], np.float32)[:, 1536:, :], (0, 2, 1))),
                  b_wdT=np.ascontiguousarray(np.transpose(
                      np.asarray(inp["L0_ffn_w_down"], np.float32), (0, 2, 1))),
                  b_sgT=_tr(inp["L0_ffn_sg_gate"]), b_suT=_tr(inp["L0_ffn_sg_up"]),
                  b_sdT=_tr(inp["L0_ffn_sg_down"])),
        "c": dict(c_wq=_tr(inp["L1_attn_Wq"]), c_wk=_tr(inp["L1_attn_Wk"]),
                  c_wv=_tr(inp["L1_attn_Wv"]),
                  c_bq=_pc(inp["L1_attn_bq"]), c_bk=_pc(inp["L1_attn_bk"]),
                  c_bv=np.ascontiguousarray(inp["L1_attn_bv"], np.float32),
                  c_maskc=mC),
        "d": dict(d_wT=_tr(inp["L1_ffn_weight"]),
                  d_alpha=np.ascontiguousarray(inp["L1_ffn_alpha"], np.float32),
                  d_bias=np.ascontiguousarray(inp["L1_ffn_bias"], np.float32)),
    }
    return base, si


def kernel(**inp):
    x = np.ascontiguousarray(np.asarray(inp["L0_attn_x"], dtype=np.float32))
    base, si = stage_input_dicts(inp)
    stages = ["a", "b", "c", "d"]
    nc = build_program(stages)
    sw = dict(base)
    for s in stages:
        sw.update(si[s])
    in_maps = [dict(sw, x0=np.ascontiguousarray(x[b])) for b in range(8)]
    res = run_bass_kernel_spmd(nc, in_maps, core_ids=list(range(8)))
    out = np.stack([res.results[b]["yout"] for b in range(8)], axis=0)
    return out.astype(np.float32)


# version 04
# speedup vs baseline: 1.0011x; 1.0011x over previous
import contextlib
import numpy as np
import ml_dtypes
import concourse.bass as bass
import concourse.mybir as mybir
from concourse.bass_utils import run_bass_kernel_spmd

F32 = mybir.dt.float32
BF16 = mybir.dt.bfloat16
AF = mybir.ActivationFunctionType
OP = mybir.AluOpType
AX = mybir.AxisListType

T = 4096
D = 1024
NT = T // 128
NS = T // 512
NC_ = D // 128
STG = 6144

ENGS = ("sync", "tensor", "vector", "scalar", "gpsimd")
DMA_ENGS = ("sync", "gpsimd")


class Prog:

    def __init__(self, nc):
        self.nc = nc
        self.cmds = {e: [] for e in ENGS}
        self.cnt = {e: 0 for e in ENGS}
        self.waited = {e: {e2: 0 for e2 in ENGS} for e in ENGS}
        self.lastw = {}
        self.lastr = {}
        self.sems = None
        self.epoch = 0

    def _wait(self, eng, src, val):
        if val <= 0:
            return
        if src == eng == "tensor":
            return
        if self.waited[eng][src] >= val:
            return
        self.waited[eng][src] = val
        self.cmds[eng].append(("w", src, val, self.epoch))

    def op(self, eng, fn, reads=(), writes=()):
        if getattr(self, "mute", False):
            return
        for k in reads:
            w = self.lastw.get(k)
            if w is not None:
                self._wait(eng, w[0], w[1])
        for k in writes:
            w = self.lastw.get(k)
            if w is not None:
                self._wait(eng, w[0], w[1])
            for e2, cv in self.lastr.get(k, {}).items():
                self._wait(eng, e2, cv)
        inc = 16 if eng in DMA_ENGS else 1
        self.cnt[eng] += inc
        self.cmds[eng].append(("i", fn, self.epoch))
        for k in writes:
            self.lastw[k] = (eng, self.cnt[eng])
            self.lastr.pop(k, None)
        for k in reads:
            self.lastr.setdefault(k, {})[eng] = self.cnt[eng]

    def barrier(self):
        snap = dict(self.cnt)
        for e in ENGS:
            for s in ENGS:
                self._wait(e, s, snap[s])

    def epoch_break(self):
        self.barrier()
        self.epoch += 1
        self.cnt = {e: 0 for e in ENGS}
        self.waited = {e: {e2: 0 for e2 in ENGS} for e in ENGS}
        self.lastw.clear()
        self.lastr.clear()

    def replay(self, block):
        bmap = {}
        for e in DMA_ENGS:
            ends = {}
            run = []
            c = 0
            ep = 0
            for cmd in self.cmds[e]:
                if cmd[-1] != ep:
                    for v in run:
                        ends[(ep, v)] = run[-1]
                    run = []
                    c = 0
                    ep = cmd[-1]
                if cmd[0] == "w":
                    for v in run:
                        ends[(ep, v)] = run[-1]
                    run = []
                else:
                    c += 16
                    run.append(c)
            for v in run:
                ends[(ep, v)] = run[-1]
            bmap[e] = ends

        def mk(engname):
            def section(E):
                count = 0
                drained = 0
                ep = 0
                for c in self.cmds[engname]:
                    if c[-1] != ep:
                        ep = c[-1]
                        count = 0
                        drained = 0
                    if c[0] == "w":
                        if engname in DMA_ENGS and count > drained:
                            E.wait_ge(self.sems[(ep, engname)], count)
                            drained = count
                        val = c[2]
                        if c[1] in DMA_ENGS:
                            val = bmap[c[1]].get((ep, val), val)
                        E.wait_ge(self.sems[(ep, c[1])], val)
                    else:
                        inst = c[1](E)
                        inc = 16 if engname in DMA_ENGS else 1
                        count += inc
                        inst.then_inc(self.sems[(ep, engname)], inc)
            return section
        block.sync(mk("sync"))
        block.tensor(mk("tensor"))
        block.vector(mk("vector"))
        block.scalar(mk("scalar"))
        block.gpsimd(mk("gpsimd"))



def load_weightT(P, w_hbm, wbf, wkey, nch, fout, wst, wstk):
    sch = max(1, STG // fout)
    c = 0
    while c < nch:
        cs = list(range(c, min(c + sch, nch)))
        for k, cc in enumerate(cs):
            P.op("sync", lambda E, cc=cc, k=k: E.dma_start(
                out=wst[:, k * fout:(k + 1) * fout],
                in_=w_hbm[cc * 128:(cc + 1) * 128, :]),
                writes=(wstk,))
        P.op("scalar", lambda E, c=c, n=len(cs): E.activation(
            out=wbf[:, c * fout:(c + n) * fout], in_=wst[:, 0:n * fout], func=AF.Copy),
            reads=(wstk,), writes=(wkey,))
        c += sch


def mk_ln_scratch(es, nc, tag):
    sm = es.enter_context(nc.sbuf_tensor(tag + "sm", [128, 1], F32))
    xc = es.enter_context(nc.sbuf_tensor(tag + "xc", [128, D], F32))
    sq = es.enter_context(nc.sbuf_tensor(tag + "sq", [128, D], F32))
    varv = es.enter_context(nc.sbuf_tensor(tag + "var", [128, 1], F32))
    stdv = es.enter_context(nc.sbuf_tensor(tag + "std", [128, 1], F32))
    rstd = es.enter_context(nc.sbuf_tensor(tag + "rstd", [128, 1], F32))
    return (sm, xc, sq, varv, stdv, rstd)


def emit_ln(P, xs, xs_key, j, out_bf, out_key, scr, sk):
    xj = xs[:, j * D:(j + 1) * D]
    sm, xc, sq, varv, stdv, rstd = scr
    P.op("vector", lambda E: E.reduce_sum(out=sm[:, :], in_=xj, axis=AX.X),
         reads=(xs_key,), writes=(sk + "sm",))
    P.op("scalar", lambda E: E.mul(out=sm[:, :], in_=sm[:, :], mul=1.0 / D),
         reads=(sk + "sm",), writes=(sk + "sm",))
    P.op("vector", lambda E: E.tensor_scalar_sub(out=xc[:, :], in0=xj, scalar1=sm[:, :]),
         reads=(xs_key, sk + "sm"), writes=(sk + "xc",))
    P.op("scalar", lambda E: E.activation(out=sq[:, :], in_=xc[:, :], func=AF.Square),
         reads=(sk + "xc",), writes=(sk + "sq",))
    P.op("vector", lambda E: E.reduce_sum(out=varv[:, :], in_=sq[:, :], axis=AX.X),
         reads=(sk + "sq",), writes=(sk + "var",))
    P.op("vector", lambda E: E.tensor_scalar(out=varv[:, :], in0=varv[:, :],
         scalar1=1.0 / D, scalar2=1e-5, op0=OP.mult, op1=OP.add),
         reads=(sk + "var",), writes=(sk + "var",))
    P.op("scalar", lambda E: E.activation(out=stdv[:, :], in_=varv[:, :], func=AF.Sqrt),
         reads=(sk + "var",), writes=(sk + "std",))
    P.op("vector", lambda E: E.reciprocal(out=rstd[:, :], in_=stdv[:, :]),
         reads=(sk + "std",), writes=(sk + "rstd",))
    P.op("vector", lambda E: E.tensor_scalar_mul(out=out_bf, in0=xc[:, :], scalar1=rstd[:, :]),
         reads=(sk + "xc", sk + "rstd"), writes=(out_key,))


def emit_transposes(P, ln_bf, ln_key, identb, psT, psT_keys, dest_fn):
    n = len(psT)
    for c in range(NC_):
        i = c % n
        P.op("tensor", lambda E, c=c, i=i: E.transpose(
            out=psT[i][:, :], in_=ln_bf[:, c * 128:(c + 1) * 128], identity=identb[:, :]),
            reads=(ln_key, "identb"), writes=(psT_keys[i],))
        dap, dkey = dest_fn(c)
        P.op("vector", lambda E, dap=dap, i=i: E.tensor_copy(out=dap, in_=psT[i][:, :]),
             reads=(psT_keys[i],), writes=(dkey,))


def bias_bcast(P, es, nc, b_hbm, name, width, ones_f):
    row = es.enter_context(nc.sbuf_tensor(name + "_row", [1, width], F32))
    bc = es.enter_context(nc.sbuf_tensor(name + "_bc", [128, width], F32))
    pb = es.enter_context(nc.psum_tensor(name + "_pb", [128, 512], F32))
    P.op("sync", lambda E: E.dma_start(out=row[:, :], in_=b_hbm[None, :]),
         writes=(name + "_row",))
    for fb in range(width // 512):
        P.op("tensor", lambda E, fb=fb: E.matmul(
            out=pb[:, :], lhsT=ones_f[0:1, 0:128], rhs=row[0:1, fb * 512:(fb + 1) * 512],
            start=True, stop=True),
            reads=(name + "_row", "ones"), writes=(name + "_pb",))
        P.op("vector", lambda E, fb=fb: E.tensor_copy(
            out=bc[:, fb * 512:(fb + 1) * 512], in_=pb[:, :]),
            reads=(name + "_pb",), writes=(name + "_bc",))
    return bc


def emit_softmax(P, pfx, ens, ensk, mx, den, attb, attbk, scale, nk=None):
    esl = ens[:, :] if nk is None else ens[:, 0:nk]
    asl = attb[:, :] if nk is None else attb[:, 0:nk]
    P.op("vector", lambda E, esl=esl: E.reduce_max(out=mx[:, :], in_=esl, axis=AX.X),
         reads=(ensk,), writes=(pfx + "mx",))
    P.op("vector", lambda E, esl=esl: E.tensor_scalar(
        out=esl, in0=esl, scalar1=mx[:, :], scalar2=scale, op0=OP.subtract, op1=OP.mult),
        reads=(ensk, pfx + "mx"), writes=(ensk,))
    P.op("scalar", lambda E, esl=esl: E.activation(out=esl, in_=esl, func=AF.Exp),
         reads=(ensk,), writes=(ensk,))
    P.op("vector", lambda E, esl=esl: E.reduce_sum(out=den[:, :], in_=esl, axis=AX.X),
         reads=(ensk,), writes=(pfx + "den",))
    P.op("vector", lambda E: E.reciprocal(out=den[:, :], in_=den[:, :]),
         reads=(pfx + "den",), writes=(pfx + "den",))
    P.op("vector", lambda E, esl=esl, asl=asl: E.tensor_scalar_mul(
        out=asl, in0=esl, scalar1=den[:, :]),
        reads=(ensk, pfx + "den"), writes=(attbk,))


def emit_ln_strip(P, x, xs, xk, lnT, lk, st, scr, sk, lnb, lnbk, identb, psT, psT_keys):
    for j in range(4):
        P.op("sync", lambda E, st=st, j=j: E.dma_start(
            out=xs[:, j * D:(j + 1) * D],
            in_=x[st * 512 + j * 128: st * 512 + (j + 1) * 128, :]),
            writes=(xk,))
    for j in range(4):
        emit_ln(P, xs, xk, j, lnb[:, :], lnbk, scr, sk)
        emit_transposes(P, lnb, lnbk, identb, psT, psT_keys,
                        lambda c, j=j: (lnT[:, c * 512 + j * 128: c * 512 + (j + 1) * 128], lk))



def emit_stage_a(nc, P, es, x, xkey, y, ykey, wq, wk, wv, wp, bq_pc, bk_pc, bv, bp,
                 masks, identb, ones_f, pfx="a"):
    qfm = nc.dram_tensor(pfx + "_qfm", [D, T], BF16, kind="Internal").ap()
    kfm = nc.dram_tensor(pfx + "_kfm", [D, T], BF16, kind="Internal").ap()
    vtm = nc.dram_tensor(pfx + "_vtm", [T, D], BF16, kind="Internal").ap()
    hk = pfx + "hbm"

    with contextlib.ExitStack() as s1:
        wst = s1.enter_context(nc.sbuf_tensor(pfx + "wst", [128, STG], F32))
        wqb = s1.enter_context(nc.sbuf_tensor(pfx + "wqb", [128, NC_ * D], BF16))
        wkb = s1.enter_context(nc.sbuf_tensor(pfx + "wkb", [128, NC_ * D], BF16))
        wvb = s1.enter_context(nc.sbuf_tensor(pfx + "wvb", [128, NC_ * D], BF16))
        load_weightT(P, wq, wqb, pfx + "wqb", NC_, D, wst, pfx + "wst")
        load_weightT(P, wk, wkb, pfx + "wkb", NC_, D, wst, pfx + "wst")
        load_weightT(P, wv, wvb, pfx + "wvb", NC_, D, wst, pfx + "wst")
        bqs = s1.enter_context(nc.sbuf_tensor(pfx + "bqs", [128, NC_], F32))
        bks = s1.enter_context(nc.sbuf_tensor(pfx + "bks", [128, NC_], F32))
        P.op("sync", lambda E: E.dma_start(out=bqs[:, :], in_=bq_pc[:, :]), writes=(pfx + "bqs",))
        P.op("sync", lambda E: E.dma_start(out=bks[:, :], in_=bk_pc[:, :]), writes=(pfx + "bks",))
        bv_bc = bias_bcast(P, s1, nc, bv, pfx + "bv", D, ones_f)

        xs = [s1.enter_context(nc.sbuf_tensor(f"{pfx}xs{i}", [128, 4 * D], F32)) for i in range(2)]
        lnb = s1.enter_context(nc.sbuf_tensor(pfx + "lnb", [128, D], BF16))
        lnT = [s1.enter_context(nc.sbuf_tensor(f"{pfx}lnT{i}", [128, NC_ * 512], BF16)) for i in range(2)]
        scr = mk_ln_scratch(s1, nc, pfx + "ln")
        psT = [s1.enter_context(nc.psum_tensor(f"{pfx}psT{i}", [128, 128], BF16)) for i in range(2)]
        pq = s1.enter_context(nc.psum_tensor(pfx + "pq", [128, 512], F32))
        pv = s1.enter_context(nc.psum_tensor(pfx + "pv", [128, 512], F32))
        ev = [s1.enter_context(nc.sbuf_tensor(f"{pfx}ev{i}", [128, 512], BF16)) for i in range(2)]

        for st in range(NS):
            b = st % 2
            xk, lk = f"{pfx}xs{b}", f"{pfx}lnT{b}"
            emit_ln_strip(P, x, xs[b], xk, lnT[b], lk, st, scr, pfx + "ln",
                          lnb, pfx + "lnb", identb, psT, [f"{pfx}psT0", f"{pfx}psT1"])
            for (wb, wkey, bcol, bkey, dst) in ((wqb, pfx + "wqb", bqs, pfx + "bqs", qfm),
                                                (wkb, pfx + "wkb", bks, pfx + "bks", kfm)):
                for fc in range(NC_):
                    for ec in range(NC_):
                        P.op("tensor", lambda E, fc=fc, ec=ec, wb=wb, b=b: E.matmul(
                            out=pq[:, :], lhsT=wb[:, ec * D + fc * 128: ec * D + fc * 128 + 128],
                            rhs=lnT[b][:, ec * 512:(ec + 1) * 512],
                            start=(ec == 0), stop=(ec == NC_ - 1)),
                            reads=(wkey, lk), writes=(pfx + "pq",))
                    eb = fc % 2
                    P.op("vector", lambda E, fc=fc, eb=eb, bcol=bcol: E.tensor_scalar_add(
                        out=ev[eb][:, :], in0=pq[:, :], scalar1=bcol[:, fc:fc + 1]),
                        reads=(pfx + "pq", bkey), writes=(f"{pfx}ev{eb}",))
                    P.op("gpsimd", lambda E, fc=fc, eb=eb, st=st, dst=dst: E.dma_start(
                        out=dst[fc * 128:(fc + 1) * 128, st * 512:(st + 1) * 512], in_=ev[eb][:, :]),
                        reads=(f"{pfx}ev{eb}",), writes=(hk,))
            for tj in range(4):
                for fb in range(2):
                    for ec in range(NC_):
                        P.op("tensor", lambda E, tj=tj, fb=fb, ec=ec, b=b: E.matmul(
                            out=pv[:, :], lhsT=lnT[b][:, ec * 512 + tj * 128: ec * 512 + tj * 128 + 128],
                            rhs=wvb[:, ec * D + fb * 512: ec * D + fb * 512 + 512],
                            start=(ec == 0), stop=(ec == NC_ - 1)),
                            reads=(pfx + "wvb", lk), writes=(pfx + "pv",))
                    eb = (tj * 2 + fb) % 2
                    P.op("vector", lambda E, fb=fb, eb=eb: E.tensor_tensor(
                        out=ev[eb][:, :], in0=pv[:, :], in1=bv_bc[:, fb * 512:(fb + 1) * 512], op=OP.add),
                        reads=(pfx + "pv", pfx + "bv_bc"), writes=(f"{pfx}ev{eb}",))
                    P.op("gpsimd", lambda E, st=st, tj=tj, fb=fb, eb=eb: E.dma_start(
                        out=vtm[st * 512 + tj * 128: st * 512 + (tj + 1) * 128, fb * 512:(fb + 1) * 512],
                        in_=ev[eb][:, :]),
                        reads=(f"{pfx}ev{eb}",), writes=(hk,))
    P.barrier()

    with contextlib.ExitStack() as s2:
        wst = s2.enter_context(nc.sbuf_tensor(pfx + "wst2", [128, STG], F32))
        wpb = s2.enter_context(nc.sbuf_tensor(pfx + "wpb", [128, NC_ * D], BF16))
        load_weightT(P, wp, wpb, pfx + "wpb", NC_, D, wst, pfx + "wst2")
        bp_bc = bias_bcast(P, s2, nc, bp, pfx + "bp", D, ones_f)
        msk = s2.enter_context(nc.sbuf_tensor(pfx + "msk", [128, 3 * 384], F32))
        P.op("sync", lambda E: E.dma_start(out=msk[:, :], in_=masks[:, :]), writes=(pfx + "msk",))

        qf = [s2.enter_context(nc.sbuf_tensor(f"{pfx}qf{i}", [128, NC_ * 128], BF16)) for i in range(2)]
        kf = [s2.enter_context(nc.sbuf_tensor(f"{pfx}kf{i}", [128, NC_ * 384], BF16)) for i in range(2)]
        vt = [s2.enter_context(nc.sbuf_tensor(f"{pfx}vt{i}", [128, 3 * D], BF16)) for i in range(2)]
        xr = [s2.enter_context(nc.sbuf_tensor(f"{pfx}xr{i}", [128, D], F32)) for i in range(2)]
        pen = [s2.enter_context(nc.psum_tensor(f"{pfx}pen{i}", [128, 384], F32)) for i in range(2)]
        ens = [s2.enter_context(nc.sbuf_tensor(f"{pfx}ens{i}", [128, 384], F32)) for i in range(2)]
        mx = [s2.enter_context(nc.sbuf_tensor(f"{pfx}mx{i}", [128, 1], F32)) for i in range(2)]
        den = [s2.enter_context(nc.sbuf_tensor(f"{pfx}den{i}", [128, 1], F32)) for i in range(2)]
        attb = [s2.enter_context(nc.sbuf_tensor(f"{pfx}attb{i}", [128, 384], BF16)) for i in range(2)]
        attT = [s2.enter_context(nc.sbuf_tensor(f"{pfx}attT{i}", [128, 3 * 128], BF16)) for i in range(2)]
        psU = [s2.enter_context(nc.psum_tensor(f"{pfx}psU{i}", [128, 128], BF16)) for i in range(2)]
        pctx = [s2.enter_context(nc.psum_tensor(f"{pfx}pctx{i}", [64, 128], F32)) for i in range(2)]
        ctxT = s2.enter_context(nc.sbuf_tensor(pfx + "ctxT", [128, NC_ * 128], BF16))
        po = s2.enter_context(nc.psum_tensor(pfx + "po", [128, 512], F32))
        oev = s2.enter_context(nc.sbuf_tensor(pfx + "oev", [128, D], F32))

        for ti in range(NT):
            b = ti % 2
            w0 = min(max(ti - 1, 0), NT - 3)
            moff = ti - w0
            for c in range(NC_):
                P.op("sync", lambda E, ti=ti, c=c, b=b: E.dma_start(
                    out=qf[b][:, c * 128:(c + 1) * 128],
                    in_=qfm[c * 128:(c + 1) * 128, ti * 128:(ti + 1) * 128]),
                    reads=(hk,), writes=(f"{pfx}qf{b}",))
                P.op("sync", lambda E, w0=w0, c=c, b=b: E.dma_start(
                    out=kf[b][:, c * 384:(c + 1) * 384],
                    in_=kfm[c * 128:(c + 1) * 128, w0 * 128: w0 * 128 + 384]),
                    reads=(hk,), writes=(f"{pfx}kf{b}",))
            for j3 in range(3):
                P.op("sync", lambda E, w0=w0, j3=j3, b=b: E.dma_start(
                    out=vt[b][:, j3 * D:(j3 + 1) * D],
                    in_=vtm[w0 * 128 + j3 * 128: w0 * 128 + (j3 + 1) * 128, :]),
                    reads=(hk,), writes=(f"{pfx}vt{b}",))
            P.op("sync", lambda E, ti=ti, b=b: E.dma_start(
                out=xr[b][:, :], in_=x[ti * 128:(ti + 1) * 128, :]),
                reads=(xkey,), writes=(f"{pfx}xr{b}",))
            for h in range(16):
                c, o = h // 2, (h % 2) * 64
                hb = h % 2
                kpen, kens, kattb, kattT, kpctx = (f"{pfx}pen{hb}", f"{pfx}ens{hb}",
                    f"{pfx}attb{hb}", f"{pfx}attT{hb}", f"{pfx}pctx{hb}")
                P.op("tensor", lambda E, c=c, o=o, b=b, hb=hb: E.matmul(
                    out=pen[hb][:, :], lhsT=qf[b][o:o + 64, c * 128:(c + 1) * 128],
                    rhs=kf[b][o:o + 64, c * 384:(c + 1) * 384], start=True, stop=True),
                    reads=(f"{pfx}qf{b}", f"{pfx}kf{b}"), writes=(kpen,))
                P.op("vector", lambda E, moff=moff, hb=hb: E.tensor_tensor(
                    out=ens[hb][:, :], in0=pen[hb][:, :], in1=msk[:, moff * 384:(moff + 1) * 384], op=OP.add),
                    reads=(kpen, pfx + "msk"), writes=(kens,))
                emit_softmax(P, f"{pfx}{hb}", ens[hb], kens, mx[hb], den[hb], attb[hb], kattb, 1.0 / 32.0)
                for k3 in range(3):
                    i = k3 % 2
                    P.op("tensor", lambda E, k3=k3, i=i, hb=hb: E.transpose(
                        out=psU[i][:, :], in_=attb[hb][:, k3 * 128:(k3 + 1) * 128], identity=identb[:, :]),
                        reads=(kattb, "identb"), writes=(f"{pfx}psU{i}",))
                    P.op("vector", lambda E, k3=k3, i=i, hb=hb: E.tensor_copy(
                        out=attT[hb][:, k3 * 128:(k3 + 1) * 128], in_=psU[i][:, :]),
                        reads=(f"{pfx}psU{i}",), writes=(kattT,))
                for k3 in range(3):
                    P.op("tensor", lambda E, k3=k3, h=h, b=b, hb=hb: E.matmul(
                        out=pctx[hb][:, :], lhsT=vt[b][:, k3 * D + h * 64: k3 * D + h * 64 + 64],
                        rhs=attT[hb][:, k3 * 128:(k3 + 1) * 128], start=(k3 == 0), stop=(k3 == 2)),
                        reads=(f"{pfx}vt{b}", kattT), writes=(kpctx,))
                P.op("vector", lambda E, c=c, o=o, hb=hb: E.tensor_copy(
                    out=ctxT[o:o + 64, c * 128:(c + 1) * 128], in_=pctx[hb][:, :]),
                    reads=(kpctx,), writes=(pfx + "ctxT",))
            for fb in range(2):
                for ec in range(NC_):
                    P.op("tensor", lambda E, fb=fb, ec=ec: E.matmul(
                        out=po[:, :], lhsT=ctxT[:, ec * 128:(ec + 1) * 128],
                        rhs=wpb[:, ec * D + fb * 512: ec * D + fb * 512 + 512],
                        start=(ec == 0), stop=(ec == NC_ - 1)),
                        reads=(pfx + "ctxT", pfx + "wpb"), writes=(pfx + "po",))
                P.op("vector", lambda E, fb=fb: E.tensor_tensor(
                    out=oev[:, fb * 512:(fb + 1) * 512], in0=po[:, :],
                    in1=bp_bc[:, fb * 512:(fb + 1) * 512], op=OP.add),
                    reads=(pfx + "po", pfx + "bp_bc"), writes=(pfx + "oev",))
            P.op("vector", lambda E, b=b: E.tensor_add(out=oev[:, :], in0=oev[:, :], in1=xr[b][:, :]),
                 reads=(pfx + "oev", f"{pfx}xr{b}"), writes=(pfx + "oev",))
            P.op("gpsimd", lambda E, ti=ti: E.dma_start(
                out=y[ti * 128:(ti + 1) * 128, :], in_=oev[:, :]),
                reads=(pfx + "oev",), writes=(ykey,))
    P.barrier()



def emit_stage_b(nc, P, es, x, xkey, y, ykey, gwT, wgT, wuT, wdT, sgT, suT, sdT,
                 identb, ones_f):
    HALF = 1536
    NH = HALF // 128
    lnh = nc.dram_tensor("b_lnT", [D, T], BF16, kind="Internal").ap()
    acc = nc.dram_tensor("b_acc", [T, D], F32, kind="Internal").ap()

    comb = es.enter_context(nc.sbuf_tensor("bcomb", [128, NT * 8], F32))

    with contextlib.ExitStack() as s0:
        gws = s0.enter_context(nc.sbuf_tensor("bgws", [128, NC_ * 8], F32))
        gwb = s0.enter_context(nc.sbuf_tensor("bgwb", [128, NC_ * 8], BF16))
        for c in range(NC_):
            P.op("sync", lambda E, c=c: E.dma_start(
                out=gws[:, c * 8:(c + 1) * 8], in_=gwT[c * 128:(c + 1) * 128, :]),
                writes=("bgws",))
        P.op("vector", lambda E: E.tensor_copy(out=gwb[:, :], in_=gws[:, :]),
             reads=("bgws",), writes=("bgwb",))
        xs = [s0.enter_context(nc.sbuf_tensor(f"bxs{i}", [128, 4 * D], F32)) for i in range(2)]
        lnb = s0.enter_context(nc.sbuf_tensor("blnb", [128, D], BF16))
        lnT = [s0.enter_context(nc.sbuf_tensor(f"blnT{i}", [128, NC_ * 512], BF16)) for i in range(2)]
        scr = mk_ln_scratch(s0, nc, "bln")
        psT = [s0.enter_context(nc.psum_tensor(f"bpsT{i}", [128, 128], BF16)) for i in range(2)]
        plog = s0.enter_context(nc.psum_tensor("bplog", [128, 8], F32))
        lg = s0.enter_context(nc.sbuf_tensor("blg", [128, 8], F32))
        m1 = s0.enter_context(nc.sbuf_tensor("bm1", [128, 1], F32))
        m2 = s0.enter_context(nc.sbuf_tensor("bm2", [128, 1], F32))
        is1 = s0.enter_context(nc.sbuf_tensor("bis1", [128, 8], F32))
        is2 = s0.enter_context(nc.sbuf_tensor("bis2", [128, 8], F32))
        msk2 = s0.enter_context(nc.sbuf_tensor("bmsk2", [128, 8], F32))
        dd = s0.enter_context(nc.sbuf_tensor("bdd", [128, 1], F32))
        w1 = s0.enter_context(nc.sbuf_tensor("bw1", [128, 1], F32))
        w2 = s0.enter_context(nc.sbuf_tensor("bw2", [128, 1], F32))
        c1 = s0.enter_context(nc.sbuf_tensor("bc1", [128, 8], F32))

        for st in range(NS):
            b = st % 2
            xk, lk = f"bxs{b}", f"blnT{b}"
            emit_ln_strip(P, x, xs[b], xk, lnT[b], lk, st, scr, "bln",
                          lnb, "blnb", identb, psT, ["bpsT0", "bpsT1"])
            for c in range(NC_):
                P.op("gpsimd", lambda E, st=st, c=c, b=b: E.dma_start(
                    out=lnh[c * 128:(c + 1) * 128, st * 512:(st + 1) * 512],
                    in_=lnT[b][:, c * 512:(c + 1) * 512]),
                    reads=(lk,), writes=("blnh",))
            for j in range(4):
                ti = st * 4 + j
                for ec in range(NC_):
                    P.op("tensor", lambda E, j=j, ec=ec, b=b: E.matmul(
                        out=plog[:, :], lhsT=lnT[b][:, ec * 512 + j * 128: ec * 512 + j * 128 + 128],
                        rhs=gwb[:, ec * 8:(ec + 1) * 8], start=(ec == 0), stop=(ec == NC_ - 1)),
                        reads=(lk, "bgwb"), writes=("bplog",))
                P.op("vector", lambda E: E.tensor_copy(out=lg[:, :], in_=plog[:, :]),
                     reads=("bplog",), writes=("blg",))
                P.op("vector", lambda E: E.reduce_max(out=m1[:, :], in_=lg[:, :], axis=AX.X),
                     reads=("blg",), writes=("bm1",))
                P.op("vector", lambda E: E.tensor_scalar(out=is1[:, :], in0=lg[:, :],
                     scalar1=m1[:, :], scalar2=None, op0=OP.is_ge),
                     reads=("blg", "bm1"), writes=("bis1",))
                P.op("vector", lambda E: E.scalar_tensor_tensor(out=msk2[:, :], in0=is1[:, :],
                     scalar=-1e30, in1=lg[:, :], op0=OP.mult, op1=OP.add),
                     reads=("bis1", "blg"), writes=("bmsk2",))
                P.op("vector", lambda E: E.reduce_max(out=m2[:, :], in_=msk2[:, :], axis=AX.X),
                     reads=("bmsk2",), writes=("bm2",))
                P.op("vector", lambda E: E.tensor_scalar(out=is2[:, :], in0=msk2[:, :],
                     scalar1=m2[:, :], scalar2=None, op0=OP.is_ge),
                     reads=("bmsk2", "bm2"), writes=("bis2",))
                P.op("vector", lambda E: E.tensor_tensor(out=dd[:, :], in0=m1[:, :], in1=m2[:, :], op=OP.subtract),
                     reads=("bm1", "bm2"), writes=("bdd",))
                P.op("scalar", lambda E: E.activation(out=w1[:, :], in_=dd[:, :], func=AF.Sigmoid),
                     reads=("bdd",), writes=("bw1",))
                P.op("vector", lambda E: E.tensor_scalar_mul(out=dd[:, :], in0=dd[:, :], scalar1=-1.0),
                     reads=("bdd",), writes=("bdd",))
                P.op("scalar", lambda E: E.activation(out=w2[:, :], in_=dd[:, :], func=AF.Sigmoid),
                     reads=("bdd",), writes=("bw2",))
                P.op("vector", lambda E: E.tensor_scalar_mul(out=c1[:, :], in0=is1[:, :], scalar1=w1[:, :]),
                     reads=("bis1", "bw1"), writes=("bc1",))
                P.op("vector", lambda E, ti=ti: E.scalar_tensor_tensor(
                    out=comb[:, ti * 8:(ti + 1) * 8], in0=is2[:, :], scalar=w2[:, :],
                    in1=c1[:, :], op0=OP.mult, op1=OP.add),
                    reads=("bis2", "bw2", "bc1"), writes=("bcomb",))
    P.barrier()

    with contextlib.ExitStack() as s1:
        wgb = s1.enter_context(nc.sbuf_tensor("bwgb", [128, NC_ * 1536], BF16))
        wub = s1.enter_context(nc.sbuf_tensor("bwub", [128, NC_ * 1536], BF16))
        wdb = s1.enter_context(nc.sbuf_tensor("bwdb", [128, NH * D], BF16))
        wst = s1.enter_context(nc.sbuf_tensor("bwst", [128, STG], F32))
        lns = [s1.enter_context(nc.sbuf_tensor(f"blns{i}", [128, NC_ * 512], BF16)) for i in range(2)]
        pg = s1.enter_context(nc.psum_tensor("bpg", [128, 512], F32))
        pu = s1.enter_context(nc.psum_tensor("bpu", [128, 512], F32))
        gs = s1.enter_context(nc.sbuf_tensor("bgs", [128, 512], F32))
        aT = s1.enter_context(nc.sbuf_tensor("baT", [128, NH * 512], BF16))
        py = s1.enter_context(nc.psum_tensor("bpy", [128, 512], F32))
        acs = [s1.enter_context(nc.sbuf_tensor(f"bacs{i}", [128, 4 * D], F32)) for i in range(2)]

        for e in [8] + list(range(8)):
            load_weightT(P, sgT if e == 8 else wgT[e], wgb, "bwgb", NC_, 1536, wst, "bwst")
            load_weightT(P, suT if e == 8 else wuT[e], wub, "bwub", NC_, 1536, wst, "bwst")
            load_weightT(P, sdT if e == 8 else wdT[e], wdb, "bwdb", NH, D, wst, "bwst")
            for st in range(NS):
                b = st % 2
                lk = f"blns{b}"
                for c in range(NC_):
                    P.op("sync", lambda E, st=st, c=c, b=b: E.dma_start(
                        out=lns[b][:, c * 512:(c + 1) * 512],
                        in_=lnh[c * 128:(c + 1) * 128, st * 512:(st + 1) * 512]),
                        reads=("blnh",), writes=(lk,))
                if e != 8:
                    for j in range(4):
                        P.op("sync", lambda E, st=st, j=j, b=b: E.dma_start(
                            out=acs[b][:, j * D:(j + 1) * D],
                            in_=acc[st * 512 + j * 128: st * 512 + (j + 1) * 128, :]),
                            reads=("bacc",), writes=(f"bacs{b}",))
                for hc in range(NH):
                    for (ps, wb, wkey, pk) in ((pg, wgb, "bwgb", "bpg"), (pu, wub, "bwub", "bpu")):
                        for ec in range(NC_):
                            P.op("tensor", lambda E, hc=hc, ec=ec, ps=ps, wb=wb, b=b: E.matmul(
                                out=ps[:, :], lhsT=wb[:, ec * 1536 + hc * 128: ec * 1536 + hc * 128 + 128],
                                rhs=lns[b][:, ec * 512:(ec + 1) * 512],
                                start=(ec == 0), stop=(ec == NC_ - 1)),
                                reads=(wkey, lk), writes=(pk,))
                    P.op("scalar", lambda E: E.activation(out=gs[:, :], in_=pg[:, :], func=AF.Silu),
                         reads=("bpg",), writes=("bgs",))
                    P.op("vector", lambda E, hc=hc: E.tensor_tensor(
                        out=aT[:, hc * 512:(hc + 1) * 512], in0=gs[:, :], in1=pu[:, :], op=OP.mult),
                        reads=("bgs", "bpu"), writes=("baT",))
                for j in range(4):
                    ti = st * 4 + j
                    for fb in range(2):
                        for hc in range(NH):
                            P.op("tensor", lambda E, j=j, fb=fb, hc=hc: E.matmul(
                                out=py[:, :], lhsT=aT[:, hc * 512 + j * 128: hc * 512 + j * 128 + 128],
                                rhs=wdb[:, hc * D + fb * 512: hc * D + fb * 512 + 512],
                                start=(hc == 0), stop=(hc == NH - 1)),
                                reads=("baT", "bwdb"), writes=("bpy",))
                        sl = (j * D + fb * 512, j * D + fb * 512 + 512)
                        if e == 8:
                            P.op("vector", lambda E, sl=sl, b=b: E.tensor_copy(
                                out=acs[b][:, sl[0]:sl[1]], in_=py[:, :]),
                                reads=("bpy",), writes=(f"bacs{b}",))
                        else:
                            P.op("vector", lambda E, sl=sl, ti=ti, e=e, b=b: E.scalar_tensor_tensor(
                                out=acs[b][:, sl[0]:sl[1]], in0=py[:, :],
                                scalar=comb[:, ti * 8 + e: ti * 8 + e + 1],
                                in1=acs[b][:, sl[0]:sl[1]], op0=OP.mult, op1=OP.add),
                                reads=("bpy", "bcomb", f"bacs{b}"), writes=(f"bacs{b}",))
                for j in range(4):
                    P.op("gpsimd", lambda E, st=st, j=j, b=b: E.dma_start(
                        out=acc[st * 512 + j * 128: st * 512 + (j + 1) * 128, :],
                        in_=acs[b][:, j * D:(j + 1) * D]),
                        reads=(f"bacs{b}",), writes=("bacc",))
    P.barrier()

    with contextlib.ExitStack() as s2:
        xs = [s2.enter_context(nc.sbuf_tensor(f"bfx{i}", [128, D], F32)) for i in range(2)]
        ac = [s2.enter_context(nc.sbuf_tensor(f"bfa{i}", [128, D], F32)) for i in range(2)]
        for ti in range(NT):
            b = ti % 2
            P.op("sync", lambda E, ti=ti, b=b: E.dma_start(
                out=xs[b][:, :], in_=x[ti * 128:(ti + 1) * 128, :]),
                reads=(xkey,), writes=(f"bfx{b}",))
            P.op("sync", lambda E, ti=ti, b=b: E.dma_start(
                out=ac[b][:, :], in_=acc[ti * 128:(ti + 1) * 128, :]),
                reads=("bacc",), writes=(f"bfa{b}",))
            P.op("vector", lambda E, b=b: E.tensor_add(out=ac[b][:, :], in0=ac[b][:, :], in1=xs[b][:, :]),
                 reads=(f"bfa{b}", f"bfx{b}"), writes=(f"bfa{b}",))
            P.op("gpsimd", lambda E, ti=ti, b=b: E.dma_start(
                out=y[ti * 128:(ti + 1) * 128, :], in_=ac[b][:, :]),
                reads=(f"bfa{b}",), writes=(ykey,))
    P.barrier()



C_PARTS = "123"
C_HALVES = "fp"


def emit_stage_c(nc, P, es, x, xkey, y, ykey, wq, wk, wv, bq_pc, bk_pc, bv, maskc,
                 identb, identf, ones_f):
    qfm = nc.dram_tensor("c_qfm", [D, T], BF16, kind="Internal").ap()
    kfm = nc.dram_tensor("c_kfm", [D, T], BF16, kind="Internal").ap()
    vtm = nc.dram_tensor("c_vtm", [T, D], BF16, kind="Internal").ap()
    SC = 0.125
    hk = "chbm"

    P.mute = "1" not in C_PARTS
    with contextlib.ExitStack() as s1:
        wst = s1.enter_context(nc.sbuf_tensor("cwst", [128, STG], F32))
        wqb = s1.enter_context(nc.sbuf_tensor("cwqb", [128, NC_ * D], BF16))
        wkb = s1.enter_context(nc.sbuf_tensor("cwkb", [128, NC_ * D], BF16))
        wvb = s1.enter_context(nc.sbuf_tensor("cwvb", [128, NC_ * D], BF16))
        load_weightT(P, wq, wqb, "cwqb", NC_, D, wst, "cwst")
        load_weightT(P, wk, wkb, "cwkb", NC_, D, wst, "cwst")
        load_weightT(P, wv, wvb, "cwvb", NC_, D, wst, "cwst")
        bqs = s1.enter_context(nc.sbuf_tensor("cbqs", [128, NC_], F32))
        bks = s1.enter_context(nc.sbuf_tensor("cbks", [128, NC_], F32))
        P.op("sync", lambda E: E.dma_start(out=bqs[:, :], in_=bq_pc[:, :]), writes=("cbqs",))
        P.op("sync", lambda E: E.dma_start(out=bks[:, :], in_=bk_pc[:, :]), writes=("cbks",))
        bv_bc = bias_bcast(P, s1, nc, bv, "cbv", D, ones_f)

        xs = [s1.enter_context(nc.sbuf_tensor(f"cxs{i}", [128, 4 * D], F32)) for i in range(2)]
        lnb = s1.enter_context(nc.sbuf_tensor("clnb", [128, D], BF16))
        lnT = [s1.enter_context(nc.sbuf_tensor(f"clnT{i}", [128, NC_ * 512], BF16)) for i in range(2)]
        scr = mk_ln_scratch(s1, nc, "cln")
        psT = [s1.enter_context(nc.psum_tensor(f"cpsT{i}", [128, 128], BF16)) for i in range(2)]
        pq = s1.enter_context(nc.psum_tensor("cpq", [128, 512], F32))
        pv = s1.enter_context(nc.psum_tensor("cpv", [128, 512], F32))
        ev = [s1.enter_context(nc.sbuf_tensor(f"cev{i}", [128, 512], BF16)) for i in range(2)]

        for st in range(NS):
            b = st % 2
            xk, lk = f"cxs{b}", f"clnT{b}"
            emit_ln_strip(P, x, xs[b], xk, lnT[b], lk, st, scr, "cln",
                          lnb, "clnb", identb, psT, ["cpsT0", "cpsT1"])
            for (wb, wkey, bcol, bkey, dst) in ((wqb, "cwqb", bqs, "cbqs", qfm),
                                                (wkb, "cwkb", bks, "cbks", kfm)):
                for fc in range(NC_):
                    for ec in range(NC_):
                        P.op("tensor", lambda E, fc=fc, ec=ec, wb=wb, b=b: E.matmul(
                            out=pq[:, :], lhsT=wb[:, ec * D + fc * 128: ec * D + fc * 128 + 128],
                            rhs=lnT[b][:, ec * 512:(ec + 1) * 512],
                            start=(ec == 0), stop=(ec == NC_ - 1)),
                            reads=(wkey, lk), writes=("cpq",))
                    eb = fc % 2
                    P.op("vector", lambda E, fc=fc, eb=eb, bcol=bcol, bkey=bkey: E.tensor_scalar_add(
                        out=ev[eb][:, :], in0=pq[:, :], scalar1=bcol[:, fc:fc + 1]),
                        reads=("cpq", bkey), writes=(f"cev{eb}",))
                    P.op("gpsimd", lambda E, fc=fc, eb=eb, st=st, dst=dst: E.dma_start(
                        out=dst[fc * 128:(fc + 1) * 128, st * 512:(st + 1) * 512], in_=ev[eb][:, :]),
                        reads=(f"cev{eb}",), writes=(hk,))
            for tj in range(4):
                for fb in range(2):
                    for ec in range(NC_):
                        P.op("tensor", lambda E, tj=tj, fb=fb, ec=ec, b=b: E.matmul(
                            out=pv[:, :], lhsT=lnT[b][:, ec * 512 + tj * 128: ec * 512 + tj * 128 + 128],
                            rhs=wvb[:, ec * D + fb * 512: ec * D + fb * 512 + 512],
                            start=(ec == 0), stop=(ec == NC_ - 1)),
                            reads=("cwvb", lk), writes=("cpv",))
                    eb = (tj * 2 + fb) % 2
                    P.op("vector", lambda E, fb=fb, eb=eb: E.tensor_tensor(
                        out=ev[eb][:, :], in0=pv[:, :], in1=bv_bc[:, fb * 512:(fb + 1) * 512], op=OP.add),
                        reads=("cpv", "cbv_bc"), writes=(f"cev{eb}",))
                    P.op("gpsimd", lambda E, st=st, tj=tj, fb=fb, eb=eb: E.dma_start(
                        out=vtm[st * 512 + tj * 128: st * 512 + (tj + 1) * 128, fb * 512:(fb + 1) * 512],
                        in_=ev[eb][:, :]),
                        reads=(f"cev{eb}",), writes=(hk,))
    P.barrier()

    P.mute = False
    P.mute = "2" not in C_PARTS
    with contextlib.ExitStack() as s2:
        msk = s2.enter_context(nc.sbuf_tensor("cmsk", [128, 384], F32))
        P.op("sync", lambda E: E.dma_start(out=msk[:, :], in_=maskc[:, :]), writes=("cmsk",))
        qf = [s2.enter_context(nc.sbuf_tensor(f"cqf{i}", [128, NC_ * 128], BF16)) for i in range(2)]
        kb_ = [s2.enter_context(nc.sbuf_tensor(f"ckb{i}", [128, NC_ * 256], BF16)) for i in range(2)]
        kg = [s2.enter_context(nc.sbuf_tensor(f"ckg{i}", [128, NC_ * 128], BF16)) for i in range(2)]
        vb_ = [s2.enter_context(nc.sbuf_tensor(f"cvb{i}", [128, 2 * D], BF16)) for i in range(2)]
        vg = [s2.enter_context(nc.sbuf_tensor(f"cvg{i}", [128, D], BF16)) for i in range(2)]
        xr = [s2.enter_context(nc.sbuf_tensor(f"cxr{i}", [128, D], F32)) for i in range(2)]
        pen = [s2.enter_context(nc.psum_tensor(f"cpen{i}", [128, 384], F32)) for i in range(2)]
        ens = [s2.enter_context(nc.sbuf_tensor(f"cens{i}", [128, 384], F32)) for i in range(2)]
        mx = [s2.enter_context(nc.sbuf_tensor(f"cmx{i}", [128, 1], F32)) for i in range(2)]
        den = [s2.enter_context(nc.sbuf_tensor(f"cden{i}", [128, 1], F32)) for i in range(2)]
        attb = [s2.enter_context(nc.sbuf_tensor(f"cattb{i}", [128, 384], BF16)) for i in range(2)]
        attT = [s2.enter_context(nc.sbuf_tensor(f"cattT{i}", [128, 3 * 128], BF16)) for i in range(2)]
        psU = [s2.enter_context(nc.psum_tensor(f"cpsU{i}", [128, 128], BF16)) for i in range(2)]
        pctx = [s2.enter_context(nc.psum_tensor(f"cpctx{i}", [64, 128], F32)) for i in range(2)]
        ctxT = s2.enter_context(nc.sbuf_tensor("cctxT", [128, NC_ * 128], F32))
        psO = [s2.enter_context(nc.psum_tensor(f"cpsO{i}", [128, 128], F32)) for i in range(2)]
        oev = s2.enter_context(nc.sbuf_tensor("coev", [128, D], F32))

        for ti in range(1, NT - 1):
            b = ti % 2
            base = ti * 128 - 64
            for c in range(NC_):
                P.op("sync", lambda E, ti=ti, c=c, b=b: E.dma_start(
                    out=qf[b][:, c * 128:(c + 1) * 128],
                    in_=qfm[c * 128:(c + 1) * 128, ti * 128:(ti + 1) * 128]),
                    reads=(hk,), writes=(f"cqf{b}",))
                P.op("sync", lambda E, base=base, c=c, b=b: E.dma_start(
                    out=kb_[b][:, c * 256:(c + 1) * 256],
                    in_=kfm[c * 128:(c + 1) * 128, base: base + 256]),
                    reads=(hk,), writes=(f"ckb{b}",))
                P.op("sync", lambda E, c=c, b=b: E.dma_start(
                    out=kg[b][:, c * 128: c * 128 + 64],
                    in_=kfm[c * 128:(c + 1) * 128, 0:64]),
                    reads=(hk,), writes=(f"ckg{b}",))
                P.op("sync", lambda E, c=c, b=b: E.dma_start(
                    out=kg[b][:, c * 128 + 64: c * 128 + 128],
                    in_=kfm[c * 128:(c + 1) * 128, T - 64: T]),
                    reads=(hk,), writes=(f"ckg{b}",))
            for j2 in range(2):
                P.op("sync", lambda E, base=base, j2=j2, b=b: E.dma_start(
                    out=vb_[b][:, j2 * D:(j2 + 1) * D],
                    in_=vtm[base + j2 * 128: base + (j2 + 1) * 128, :]),
                    reads=(hk,), writes=(f"cvb{b}",))
            P.op("sync", lambda E, b=b: E.dma_start(
                out=vg[b][0:64, :], in_=vtm[0:64, :]),
                reads=(hk,), writes=(f"cvg{b}",))
            P.op("sync", lambda E, b=b: E.dma_start(
                out=vg[b][64:128, :], in_=vtm[T - 64: T, :]),
                reads=(hk,), writes=(f"cvg{b}",))
            P.op("sync", lambda E, ti=ti, b=b: E.dma_start(
                out=xr[b][:, :], in_=x[ti * 128:(ti + 1) * 128, :]),
                reads=(xkey,), writes=(f"cxr{b}",))
            for h in range(16):
                c, o = h // 2, (h % 2) * 64
                hb = h % 2
                kpen, kens, kattb, kattT, kpctx = (f"cpen{hb}", f"cens{hb}",
                    f"cattb{hb}", f"cattT{hb}", f"cpctx{hb}")
                P.op("tensor", lambda E, c=c, o=o, b=b, hb=hb: E.matmul(
                    out=pen[hb][:, 0:256], lhsT=qf[b][o:o + 64, c * 128:(c + 1) * 128],
                    rhs=kb_[b][o:o + 64, c * 256:(c + 1) * 256], start=True, stop=True),
                    reads=(f"cqf{b}", f"ckb{b}"), writes=(kpen,))
                P.op("tensor", lambda E, c=c, o=o, b=b, hb=hb: E.matmul(
                    out=pen[hb][:, 256:384], lhsT=qf[b][o:o + 64, c * 128:(c + 1) * 128],
                    rhs=kg[b][o:o + 64, c * 128:(c + 1) * 128], start=True, stop=True),
                    reads=(f"cqf{b}", f"ckg{b}"), writes=(kpen,))
                P.op("vector", lambda E, hb=hb: E.tensor_tensor(
                    out=ens[hb][:, :], in0=pen[hb][:, :], in1=msk[:, :], op=OP.add),
                    reads=(kpen, "cmsk"), writes=(kens,))
                emit_softmax(P, f"c{hb}", ens[hb], kens, mx[hb], den[hb], attb[hb], kattb, SC)
                for k3 in range(3):
                    i = k3 % 2
                    P.op("tensor", lambda E, k3=k3, i=i, hb=hb: E.transpose(
                        out=psU[i][:, :], in_=attb[hb][:, k3 * 128:(k3 + 1) * 128], identity=identb[:, :]),
                        reads=(kattb, "identb"), writes=(f"cpsU{i}",))
                    P.op("vector", lambda E, k3=k3, i=i, hb=hb: E.tensor_copy(
                        out=attT[hb][:, k3 * 128:(k3 + 1) * 128], in_=psU[i][:, :]),
                        reads=(f"cpsU{i}",), writes=(kattT,))
                for k2 in range(2):
                    P.op("tensor", lambda E, k2=k2, h=h, b=b, hb=hb: E.matmul(
                        out=pctx[hb][:, :], lhsT=vb_[b][:, k2 * D + h * 64: k2 * D + h * 64 + 64],
                        rhs=attT[hb][:, k2 * 128:(k2 + 1) * 128], start=(k2 == 0), stop=False),
                        reads=(f"cvb{b}", kattT), writes=(kpctx,))
                P.op("tensor", lambda E, h=h, b=b, hb=hb: E.matmul(
                    out=pctx[hb][:, :], lhsT=vg[b][:, h * 64: h * 64 + 64],
                    rhs=attT[hb][:, 256:384], start=False, stop=True),
                    reads=(f"cvg{b}", kattT), writes=(kpctx,))
                P.op("vector", lambda E, c=c, o=o, hb=hb: E.tensor_copy(
                    out=ctxT[o:o + 64, c * 128:(c + 1) * 128], in_=pctx[hb][:, :]),
                    reads=(kpctx,), writes=("cctxT",))
            for c in range(NC_):
                i = c % 2
                P.op("tensor", lambda E, c=c, i=i: E.transpose(
                    out=psO[i][:, :], in_=ctxT[:, c * 128:(c + 1) * 128], identity=identf[:, :]),
                    reads=("cctxT", "identf"), writes=(f"cpsO{i}",))
                P.op("vector", lambda E, c=c, i=i, b=b: E.tensor_tensor(
                    out=oev[:, c * 128:(c + 1) * 128], in0=psO[i][:, :],
                    in1=xr[b][:, c * 128:(c + 1) * 128], op=OP.add),
                    reads=(f"cpsO{i}", f"cxr{b}"), writes=("coev",))
            P.op("gpsimd", lambda E, ti=ti: E.dma_start(
                out=y[ti * 128:(ti + 1) * 128, :], in_=oev[:, :]),
                reads=("coev",), writes=(ykey,))
    P.barrier()

    P.mute = False
    P.mute = "3" not in C_PARTS
    with contextlib.ExitStack() as s3:
        kall = s3.enter_context(nc.sbuf_tensor("ckall", [128, NC_ * T], BF16))
        vall = s3.enter_context(nc.sbuf_tensor("cvall", [128, NT * D], BF16))
        for c in range(NC_):
            P.op("sync", lambda E, c=c: E.dma_start(
                out=kall[:, c * T:(c + 1) * T], in_=kfm[c * 128:(c + 1) * 128, :]),
                reads=(hk,), writes=("ckall",))
        for j in range(NT):
            P.op("sync", lambda E, j=j: E.dma_start(
                out=vall[:, j * D:(j + 1) * D], in_=vtm[j * 128:(j + 1) * 128, :]),
                reads=(hk,), writes=("cvall",))
        qfe = s3.enter_context(nc.sbuf_tensor("ceqf", [128, NC_ * 128], BF16))
        xre = s3.enter_context(nc.sbuf_tensor("cexr", [128, D], F32))
        pE = s3.enter_context(nc.psum_tensor("cpE", [128, 512], F32))
        enb = s3.enter_context(nc.sbuf_tensor("cenb", [128, T], F32))
        mxe = s3.enter_context(nc.sbuf_tensor("cemx", [128, 1], F32))
        dene = s3.enter_context(nc.sbuf_tensor("ceden", [128, 1], F32))
        attbe = s3.enter_context(nc.sbuf_tensor("ceattb", [128, T], BF16))
        attTe = s3.enter_context(nc.sbuf_tensor("ceattT", [128, T], BF16))
        psE = [s3.enter_context(nc.psum_tensor(f"cpsE{i}", [128, 128], BF16)) for i in range(2)]
        pctxe = s3.enter_context(nc.psum_tensor("cepctx", [64, 128], F32))
        ctxTe = s3.enter_context(nc.sbuf_tensor("cectxT", [128, NC_ * 128], F32))
        psOe = [s3.enter_context(nc.psum_tensor(f"cpsF{i}", [128, 128], F32)) for i in range(2)]
        oeve = s3.enter_context(nc.sbuf_tensor("ceoev", [128, D], F32))

        for ei, ti in enumerate((0, NT - 1)):
            for c in range(NC_):
                P.op("sync", lambda E, ti=ti, c=c: E.dma_start(
                    out=qfe[:, c * 128:(c + 1) * 128],
                    in_=qfm[c * 128:(c + 1) * 128, ti * 128:(ti + 1) * 128]),
                    reads=(hk,), writes=("ceqf",))
            P.op("sync", lambda E, ti=ti: E.dma_start(
                out=xre[:, :], in_=x[ti * 128:(ti + 1) * 128, :]),
                reads=(xkey,), writes=("cexr",))
            for h in range(16):
                c, o = h // 2, (h % 2) * 64
                for ks in range(NC_):
                    P.op("tensor", lambda E, c=c, o=o, ks=ks: E.matmul(
                        out=pE[:, :], lhsT=qfe[o:o + 64, c * 128:(c + 1) * 128],
                        rhs=kall[o:o + 64, c * T + ks * 512: c * T + (ks + 1) * 512],
                        start=True, stop=True),
                        reads=("ceqf", "ckall"), writes=("cpE",))
                    P.op("vector", lambda E, ks=ks: E.tensor_copy(
                        out=enb[:, ks * 512:(ks + 1) * 512], in_=pE[:, :]),
                        reads=("cpE",), writes=("cenb",))
                if ti == 0:
                    P.op("vector", lambda E: E.memset(enb[64:128, 192:4032], -1e30),
                         reads=(), writes=("cenb",))
                else:
                    P.op("vector", lambda E: E.memset(enb[0:64, 64:3904], -1e30),
                         reads=(), writes=("cenb",))
                emit_softmax(P, "ce", enb, "cenb", mxe, dene, attbe, "ceattb", SC)
                for kc in range(NT):
                    i = kc % 2
                    P.op("tensor", lambda E, kc=kc, i=i: E.transpose(
                        out=psE[i][:, :], in_=attbe[:, kc * 128:(kc + 1) * 128],
                        identity=identb[:, :]),
                        reads=("ceattb", "identb"), writes=(f"cpsE{i}",))
                    P.op("vector", lambda E, kc=kc, i=i: E.tensor_copy(
                        out=attTe[:, kc * 128:(kc + 1) * 128], in_=psE[i][:, :]),
                        reads=(f"cpsE{i}",), writes=("cattTe",))
                for kc in range(NT):
                    P.op("tensor", lambda E, kc=kc, h=h: E.matmul(
                        out=pctxe[:, :], lhsT=vall[:, kc * D + h * 64: kc * D + h * 64 + 64],
                        rhs=attTe[:, kc * 128:(kc + 1) * 128],
                        start=(kc == 0), stop=(kc == NT - 1)),
                        reads=("cvall", "cattTe"), writes=("cepctx",))
                P.op("vector", lambda E, c=c, o=o: E.tensor_copy(
                    out=ctxTe[o:o + 64, c * 128:(c + 1) * 128], in_=pctxe[:, :]),
                    reads=("cepctx",), writes=("cectxT",))
            for c in range(NC_):
                i = c % 2
                P.op("tensor", lambda E, c=c, i=i: E.transpose(
                    out=psOe[i][:, :], in_=ctxTe[:, c * 128:(c + 1) * 128], identity=identf[:, :]),
                    reads=("cectxT", "identf"), writes=(f"cpsF{i}",))
                P.op("vector", lambda E, c=c, i=i: E.tensor_tensor(
                    out=oeve[:, c * 128:(c + 1) * 128], in0=psOe[i][:, :],
                    in1=xre[:, c * 128:(c + 1) * 128], op=OP.add),
                    reads=(f"cpsF{i}", "cexr"), writes=("ceoev",))
            P.op("gpsimd", lambda E, ti=ti: E.dma_start(
                out=y[ti * 128:(ti + 1) * 128, :], in_=oeve[:, :]),
                reads=("ceoev",), writes=(ykey,))
    P.mute = False
    P.barrier()



def emit_stage_d(nc, P, es, x, xkey, y, ykey, wT, alpha, bias, identb, ones_f):
    wbf = es.enter_context(nc.sbuf_tensor("dwbf", [128, NC_ * D], BF16))
    bias_bc = bias_bcast(P, es, nc, bias, "dbb", D, ones_f)
    with contextlib.ExitStack() as s0:
        wst = s0.enter_context(nc.sbuf_tensor("dwst", [128, NC_ * D], F32))
        t1 = s0.enter_context(nc.sbuf_tensor("dt1", [128, NC_ * D], F32))
        t2 = s0.enter_context(nc.sbuf_tensor("dt2", [128, NC_ * D], F32))
        rs = s0.enter_context(nc.sbuf_tensor("drs", [128, 1], F32))
        ptot = s0.enter_context(nc.psum_tensor("dptot", [1, 1], F32))
        dsc = s0.enter_context(nc.sbuf_tensor("ddsc", [1, 1], F32))
        pcol = s0.enter_context(nc.psum_tensor("dpcol", [128, 1], F32))
        dcol = s0.enter_context(nc.sbuf_tensor("ddcol", [128, 1], F32))
        ncol = s0.enter_context(nc.sbuf_tensor("dncol", [128, 1], F32))
        asb = s0.enter_context(nc.sbuf_tensor("dasb", [1, 1], F32))
        acol = s0.enter_context(nc.sbuf_tensor("dacol", [128, 1], F32))

        for c in range(NC_):
            P.op("sync", lambda E, c=c: E.dma_start(
                out=wst[:, c * D:(c + 1) * D], in_=wT[c * 128:(c + 1) * 128, :]),
                writes=("dwst",))
        P.op("sync", lambda E: E.dma_start(out=asb[:, :], in_=alpha[None, :]), writes=("dasb",))
        P.op("vector", lambda E: E.reduce_sum(out=rs[:, :], in_=wst[:, :], axis=AX.X,
             apply_absolute_value=True),
             reads=("dwst",), writes=("drs",))
        P.op("tensor", lambda E: E.matmul(out=ptot[:, :], lhsT=ones_f[0:128, 0:1], rhs=rs[:, :],
             start=True, stop=True),
             reads=("drs", "ones"), writes=("dptot",))
        P.op("scalar", lambda E: E.mul(out=dsc[:, :], in_=ptot[:, :], mul=0.7 / (D * D)),
             reads=("dptot",), writes=("ddsc",))
        P.op("tensor", lambda E: E.matmul(out=pcol[:, :], lhsT=ones_f[0:1, 0:128], rhs=dsc[:, :],
             start=True, stop=True),
             reads=("ddsc", "ones"), writes=("dpcol",))
        P.op("vector", lambda E: E.tensor_copy(out=dcol[:, :], in_=pcol[:, :]),
             reads=("dpcol",), writes=("ddcol",))
        P.op("vector", lambda E: E.tensor_scalar_mul(out=ncol[:, :], in0=dcol[:, :], scalar1=-1.0),
             reads=("ddcol",), writes=("dncol",))
        P.op("tensor", lambda E: E.matmul(out=pcol[:, :], lhsT=ones_f[0:1, 0:128], rhs=asb[:, :],
             start=True, stop=True),
             reads=("dasb", "ones"), writes=("dpcol",))
        P.op("vector", lambda E: E.tensor_copy(out=acol[:, :], in_=pcol[:, :]),
             reads=("dpcol",), writes=("dacol",))
        P.op("vector", lambda E: E.tensor_scalar(out=t1[:, :], in0=wst[:, :], scalar1=dcol[:, :],
             scalar2=None, op0=OP.is_gt),
             reads=("dwst", "ddcol"), writes=("dt1",))
        P.op("vector", lambda E: E.tensor_scalar(out=t2[:, :], in0=wst[:, :], scalar1=ncol[:, :],
             scalar2=None, op0=OP.is_lt),
             reads=("dwst", "dncol"), writes=("dt2",))
        P.op("vector", lambda E: E.tensor_sub(out=t1[:, :], in0=t1[:, :], in1=t2[:, :]),
             reads=("dt1", "dt2"), writes=("dt1",))
        P.op("vector", lambda E: E.tensor_scalar_mul(out=wbf[:, :], in0=t1[:, :], scalar1=acol[:, :]),
             reads=("dt1", "dacol"), writes=("dwbf",))
    P.barrier()

    with contextlib.ExitStack() as s1:
        xs = [s1.enter_context(nc.sbuf_tensor(f"dxs{i}", [128, 4 * D], F32)) for i in range(2)]
        lnb = s1.enter_context(nc.sbuf_tensor("dlnb", [128, D], BF16))
        lnT = [s1.enter_context(nc.sbuf_tensor(f"dlnT{i}", [128, NC_ * 512], BF16)) for i in range(2)]
        scr = mk_ln_scratch(s1, nc, "dln")
        psT = [s1.enter_context(nc.psum_tensor(f"dpsT{i}", [128, 128], BF16)) for i in range(2)]
        po = s1.enter_context(nc.psum_tensor("dpo", [128, 512], F32))
        oev = [s1.enter_context(nc.sbuf_tensor(f"doev{i}", [128, D], F32)) for i in range(2)]

        for st in range(NS):
            b = st % 2
            xk, lk = f"dxs{b}", f"dlnT{b}"
            emit_ln_strip(P, x, xs[b], xk, lnT[b], lk, st, scr, "dln",
                          lnb, "dlnb", identb, psT, ["dpsT0", "dpsT1"])
            for j in range(4):
                ob = j % 2
                for fb in range(2):
                    for ec in range(NC_):
                        P.op("tensor", lambda E, j=j, fb=fb, ec=ec, b=b: E.matmul(
                            out=po[:, :], lhsT=lnT[b][:, ec * 512 + j * 128: ec * 512 + j * 128 + 128],
                            rhs=wbf[:, ec * D + fb * 512: ec * D + fb * 512 + 512],
                            start=(ec == 0), stop=(ec == NC_ - 1)),
                            reads=(lk, "dwbf"), writes=("dpo",))
                    P.op("vector", lambda E, fb=fb, ob=ob: E.tensor_tensor(
                        out=oev[ob][:, fb * 512:(fb + 1) * 512], in0=po[:, :],
                        in1=bias_bc[:, fb * 512:(fb + 1) * 512], op=OP.add),
                        reads=("dpo", "dbb_bc"), writes=(f"doev{ob}",))
                P.op("vector", lambda E, j=j, ob=ob, b=b: E.tensor_add(
                    out=oev[ob][:, :], in0=oev[ob][:, :], in1=xs[b][:, j * D:(j + 1) * D]),
                    reads=(f"doev{ob}", xk), writes=(f"doev{ob}",))
                P.op("gpsimd", lambda E, st=st, j=j, ob=ob: E.dma_start(
                    out=y[st * 512 + j * 128: st * 512 + (j + 1) * 128, :], in_=oev[ob][:, :]),
                    reads=(f"doev{ob}",), writes=(ykey,))
    P.barrier()



def _consts():
    identb = np.eye(128, dtype=ml_dtypes.bfloat16)
    identf = np.eye(128, dtype=np.float32)
    ones = np.ones((128, 128), dtype=np.float32)
    i = np.arange(128)[:, None]
    j = np.arange(384)[None, :]
    mA = np.concatenate(
        [np.where(np.abs(v * 128 + i - j) <= 128, 0.0, -1e30).astype(np.float32)
         for v in range(3)], axis=1)
    mC = np.zeros((128, 384), dtype=np.float32)
    jb = np.arange(256)[None, :] // 64
    lo = (np.arange(128)[:, None] >= 64).astype(np.int64)
    valid = (jb >= lo) & (jb <= lo + 2)
    mC[:, 0:256] = np.where(valid, 0.0, np.float32(-1e30))
    col = np.arange(4096)[None, :]
    row = np.arange(128)[:, None]
    mE0 = np.where((row >= 64) & (col >= 192) & (col < 4032), np.float32(-1e30), 0.0).astype(np.float32)
    mE1 = np.where((row < 64) & (col >= 64) & (col < 3904), np.float32(-1e30), 0.0).astype(np.float32)
    mE = np.concatenate([mE0, mE1], axis=1)
    return identb, identf, ones, mA, mC, mE


def build_program(stages):
    nc = bass.Bass("TRN2", target_bir_lowering=False, debug=False)
    P = Prog(nc)
    din = lambda n, s: nc.dram_tensor(n, s, F32, kind="ExternalInput").ap()

    identb_t = nc.dram_tensor("identb", [128, 128], BF16, kind="ExternalInput").ap()
    identf_t = din("identf", [128, 128])
    ones_t = din("ones", [128, 128])

    x0 = din("x0", [T, D])
    yout = nc.dram_tensor("yout", [T, D], F32, kind="ExternalOutput").ap()

    with contextlib.ExitStack() as es:
        identb = es.enter_context(nc.sbuf_tensor("sidentb", [128, 128], BF16))
        identf = es.enter_context(nc.sbuf_tensor("sidentf", [128, 128], F32))
        ones_f = es.enter_context(nc.sbuf_tensor("sones", [128, 128], F32))
        P.op("sync", lambda E: E.dma_start(out=identb[:, :], in_=identb_t[:, :]), writes=("identb",))
        P.op("sync", lambda E: E.dma_start(out=identf[:, :], in_=identf_t[:, :]), writes=("identf",))
        P.op("sync", lambda E: E.dma_start(out=ones_f[:, :], in_=ones_t[:, :]), writes=("ones",))

        cur, ck = x0, "x_in"
        for si, stg in enumerate(stages):
            last = si == len(stages) - 1
            if last:
                nxt, nk = yout, "y_out"
            else:
                nxt = nc.dram_tensor(f"mid{si}", [T, D], F32, kind="Internal").ap()
                nk = f"mid{si}"
            if stg == "a":
                emit_stage_a(nc, P, es, cur, ck, nxt, nk,
                             din("a_wq", [D, D]), din("a_wk", [D, D]), din("a_wv", [D, D]),
                             din("a_wp", [D, D]), din("a_bq", [128, NC_]), din("a_bk", [128, NC_]),
                             din("a_bv", [D]), din("a_bp", [D]), din("a_masks", [128, 3 * 384]),
                             identb, ones_f)
            elif stg == "b":
                emit_stage_b(nc, P, es, cur, ck, nxt, nk,
                             din("b_gwT", [D, 8]), din("b_wgT", [8, D, 1536]), din("b_wuT", [8, D, 1536]),
                             din("b_wdT", [8, 1536, D]), din("b_sgT", [D, 1536]), din("b_suT", [D, 1536]),
                             din("b_sdT", [1536, D]), identb, ones_f)
            elif stg == "c":
                emit_stage_c(nc, P, es, cur, ck, nxt, nk,
                             din("c_wq", [D, D]), din("c_wk", [D, D]), din("c_wv", [D, D]),
                             din("c_bq", [128, NC_]), din("c_bk", [128, NC_]), din("c_bv", [D]),
                             din("c_maskc", [128, 384]), identb, identf, ones_f)
            elif stg == "d":
                emit_stage_d(nc, P, es, cur, ck, nxt, nk,
                             din("d_wT", [D, D]), din("d_alpha", [1]), din("d_bias", [D]),
                             identb, ones_f)
            cur, ck = nxt, nk
            if not last:
                P.epoch_break()

        P.barrier()
        P.sems = {(ep, e): es.enter_context(nc.semaphore(f"sem{ep}_{e}"))
                  for ep in range(P.epoch + 1) for e in ENGS}
        with nc.Block() as block:
            P.replay(block)
    return nc


def _pc(b):
    return np.ascontiguousarray(np.asarray(b, dtype=np.float32).reshape(NC_, 128).T)


def _tr(w):
    return np.ascontiguousarray(np.asarray(w, dtype=np.float32).T)


def stage_input_dicts(inp):
    identb, identf, ones, mA, mC, mE = _consts()
    base = {"identb": identb, "identf": identf, "ones": ones}
    si = {
        "a": dict(a_wq=_tr(inp["L0_attn_Wq"]), a_wk=_tr(inp["L0_attn_Wk"]),
                  a_wv=_tr(inp["L0_attn_Wv"]), a_wp=_tr(inp["L0_attn_Wp"]),
                  a_bq=_pc(inp["L0_attn_bq"]), a_bk=_pc(inp["L0_attn_bk"]),
                  a_bv=np.ascontiguousarray(inp["L0_attn_bv"], np.float32),
                  a_bp=np.ascontiguousarray(inp["L0_attn_bp"], np.float32),
                  a_masks=mA),
        "b": dict(b_gwT=_tr(inp["L0_ffn_gate_w"]),
                  b_wgT=np.ascontiguousarray(np.transpose(
                      np.asarray(inp["L0_ffn_w_up"], np.float32)[:, :1536, :], (0, 2, 1))),
                  b_wuT=np.ascontiguousarray(np.transpose(
                      np.asarray(inp["L0_ffn_w_up"], np.float32)[:, 1536:, :], (0, 2, 1))),
                  b_wdT=np.ascontiguousarray(np.transpose(
                      np.asarray(inp["L0_ffn_w_down"], np.float32), (0, 2, 1))),
                  b_sgT=_tr(inp["L0_ffn_sg_gate"]), b_suT=_tr(inp["L0_ffn_sg_up"]),
                  b_sdT=_tr(inp["L0_ffn_sg_down"])),
        "c": dict(c_wq=_tr(inp["L1_attn_Wq"]), c_wk=_tr(inp["L1_attn_Wk"]),
                  c_wv=_tr(inp["L1_attn_Wv"]),
                  c_bq=_pc(inp["L1_attn_bq"]), c_bk=_pc(inp["L1_attn_bk"]),
                  c_bv=np.ascontiguousarray(inp["L1_attn_bv"], np.float32),
                  c_maskc=mC),
        "d": dict(d_wT=_tr(inp["L1_ffn_weight"]),
                  d_alpha=np.ascontiguousarray(inp["L1_ffn_alpha"], np.float32),
                  d_bias=np.ascontiguousarray(inp["L1_ffn_bias"], np.float32)),
    }
    return base, si


def kernel(**inp):
    x = np.ascontiguousarray(np.asarray(inp["L0_attn_x"], dtype=np.float32))
    base, si = stage_input_dicts(inp)
    stages = ["a", "b", "c", "d"]
    nc = build_program(stages)
    sw = dict(base)
    for s in stages:
        sw.update(si[s])
    in_maps = [dict(sw, x0=np.ascontiguousarray(x[b])) for b in range(8)]
    res = run_bass_kernel_spmd(nc, in_maps, core_ids=list(range(8)))
    out = np.stack([res.results[b]["yout"] for b in range(8)], axis=0)
    return out.astype(np.float32)
